# version 05
# baseline (speedup 1.0000x reference)
import math
import os

import numpy as np
import concourse.bass as bass
import concourse.mybir as mybir
from concourse.bass import IndirectOffsetOnAxis
from concourse.bass_utils import run_bass_kernel_spmd

F32 = mybir.dt.float32
BF16 = mybir.dt.bfloat16
FP8 = mybir.dt.float8e4
I32 = mybir.dt.int32
AF = mybir.ActivationFunctionType
ALU = mybir.AluOpType
AX = mybir.AxisListType.X

NCORE = 8
B, S, D = 4, 2048, 2048
NH, DH = 16, 128
HM = 8
T = 1024
TC = T // 128
DC = D // 128
KC = S // 128
IF = 5632
IC = IF // 128
IPAD = 6144
ISH = IPAD // NCORE
HSH = D // NCORE
FG = 2
G = T // FG
GC = G // 128
EPS = 1e-5
RSQ = 1.0 / math.sqrt(128.0)
LN1E4 = math.log(10000.0)
TWO_PI = 2.0 * math.pi

ENGINES = ("sync", "gpsimd", "vector", "scalar", "tensor")


class Prog:

    def __init__(self, nc):
        self.nc = nc
        self.ops = {e: [] for e in ENGINES}
        self.count = {e: 0 for e in ENGINES}
        self.sems = {}
        self._cms = []
        self.waited = {e: set() for e in ENGINES}
        for e in ENGINES:
            cm = nc.semaphore(f"pg_{e}")
            self.sems[e] = cm.__enter__()
            self._cms.append(cm)

    def stream(self, name):
        cm = self.nc.semaphore(f"st_{name}")
        self.sems[name] = cm.__enter__()
        self._cms.append(cm)
        self.count[name] = 0
        return name

    def emit(self, eng, fn, dma=False, waits=(), sem=None):
        key = sem if sem is not None else eng
        step = 16 if dma else 1
        self.count[key] += step
        tick = (key, self.count[key])
        waits = tuple(w for w in waits if w is not None)
        for (wk, wv) in waits:
            if wk in ENGINES:
                self.waited[wk].add(wv)
        self.ops[eng].append((fn, key, step, self.count[key], waits))
        return tick

    def wait_only(self, eng, waits):
        waits = tuple(w for w in waits if w is not None)
        for (wk, wv) in waits:
            if wk in ENGINES:
                self.waited[wk].add(wv)
        self.ops[eng].append((None, None, 0, 0, waits))

    def total(self, key):
        return (key, self.count[key])

    def barrier(self):
        totals = [(k, v) for k, v in self.count.items() if v > 0]
        for e in ENGINES:
            self.wait_only(e, [t for t in totals if t[0] != e])

    def install(self, block):
        remap = {}
        for e in ENGINES:
            vals = sorted(self.waited[e])
            remap[e] = {v: i + 1 for i, v in enumerate(vals)}
        sems = self.sems

        def map_wait(wk, wv):
            if wk in ENGINES:
                vals = remap[wk]
                if wv in vals:
                    return vals[wv]
                bigger = [nv for ov, nv in vals.items() if ov >= wv]
                return min(bigger) if bigger else len(vals)
            return wv

        for e in ENGINES:
            ops = self.ops[e]
            rm = remap[e]

            def make(ops, rm, e):
                def body(engine):
                    for (fn, key, step, tick, waits) in ops:
                        for (wk, wv) in waits:
                            engine.wait_ge(sems[wk], map_wait(wk, wv))
                        if fn is None:
                            continue
                        inst = fn(engine)
                        if inst is None:
                            continue
                        if key in ENGINES:
                            if tick in self.waited[key]:
                                inst.then_inc(sems[key], 1)
                        else:
                            inst.then_inc(sems[key], step)
                return body
            getattr(block, e)(make(ops, rm, e))


class MK:
    def __init__(self, n_sub=4):
        self.n_sub = n_sub
        nc = bass.Bass("TRN2", target_bir_lowering=False, debug=False)
        self.nc = nc
        self._guards = []

        dr = lambda *a, **k: nc.dram_tensor(*a, **k).ap()
        self.x_in = dr("x_in", [T, D], F32, kind="ExternalInput")
        self.y_out = dr("y_out", [T, D], F32, kind="ExternalOutput")
        self.win_s = dr("win_s", [D, 3 * HM * DH], F32, kind="ExternalInput")
        self.lnw_in = dr("lnw_in", [1, D], F32, kind="ExternalInput")
        self.lnb_in = dr("lnb_in", [1, D], F32, kind="ExternalInput")
        self.wout_a = dr("wout_a", [D, D], F32, kind="ExternalInput")
        self.wqkvT_s = dr("wqkvT_s", [D, 3 * HM * DH], F32, kind="ExternalInput")
        self.woutT_b = dr("woutT_b", [D, D], F32, kind="ExternalInput")
        self.wg_s = dr("wg_s", [ISH, D], F32, kind="ExternalInput")
        self.wu_s = dr("wu_s", [ISH, D], F32, kind="ExternalInput")
        self.wd_s = dr("wd_s", [HSH, IF], F32, kind="ExternalInput")
        self.cidx_in = dr("cidx_in", [128, NH], I32, kind="ExternalInput")

        self.x_spill = dr("x_spill", [128, TC * D], F32)
        self.xn_bin = nc.dram_tensor("xn_bin", [DC, 128, T], BF16)
        self.xn_bout = nc.dram_tensor("xn_bout", [2, DC, 128, T], BF16)
        self.ctx_bin = nc.dram_tensor("ctx_bin", [HM, 128, S], BF16)
        self.ctx_bout = nc.dram_tensor("ctx_bout", [2 * HM * 128 * 2, T], BF16)
        self.g_bin = nc.dram_tensor("g_bin", [D, ISH], FP8)
        self.g_bout = nc.dram_tensor("g_bout", [NCORE, D, ISH], FP8)
        self.u_bin = nc.dram_tensor("u_bin", [D, ISH], FP8)
        self.u_bout = nc.dram_tensor("u_bout", [NCORE, D, ISH], FP8)
        self.d_bin = nc.dram_tensor("d_bin", [IF, HSH], FP8)
        self.d_bout = nc.dram_tensor("d_bout", [NCORE, IF, HSH], FP8)
        self.wsg_bin = nc.dram_tensor("wsg_bin", [ISH, 1], F32)
        self.wsg_bout = nc.dram_tensor("wsg_bout", [IPAD, 1], F32)
        self.wsu_bin = nc.dram_tensor("wsu_bin", [ISH, 1], F32)
        self.wsu_bout = nc.dram_tensor("wsu_bout", [IPAD, 1], F32)
        self.wsd_bin = nc.dram_tensor("wsd_bin", [HSH, 1], F32)
        self.wsd_bout = nc.dram_tensor("wsd_bout", [D, 1], F32)
        self.vb = nc.dram_tensor("vb", [16, 2 * S], F32)
        self.rope_dram = nc.dram_tensor("rope_dram", [2, 128, S], F32)
        self.dbg = nc.dram_tensor("dbg", [8, 128, S], F32)
        if os.environ.get("MK_DEBUG") or os.environ.get("MK_DEBUG2"):
            self.dbg_out = dr("dbg_out", [8, 128, S], F32, kind="ExternalOutput")
            self.xn_dbg = dr("xn_dbg", [2, DC, 128, T], BF16, kind="ExternalOutput")
            self.ctx_dbg = dr("ctx_dbg", [2 * HM * 128 * 2, T], BF16,
                              kind="ExternalOutput")

        self.PAIRS = [[0, 1], [2, 3], [4, 5], [6, 7]]
        self.ALL8 = [list(range(NCORE))]

        self.x_res = self._sb("x_res", [128, TC * D], F32)
        self.ident_bf = self._sb("ident_bf", [128, 128], BF16)
        self.ident_f8 = self._sb("ident_f8", [128, 128], FP8)
        self.ident_f32 = self._sb("ident_f32", [128, 128], F32)
        self.ones_col_bf = self._sb("ones_col_bf", [128, 1], BF16)
        self.stats = self._sb("stats", [128, 40], F32)
        self.svec = self._sb("svec", [128, 32], F32)
        self.tiny = self._sb("tiny", [1, 64], F32)
        self.cidx = self._sb("cidx", [128, NH], I32)
        self.scrD = self._sb("scrD", [128, D], F32)
        self.scr512 = self._sb("scr512", [128, 512], F32)

        self.block_cm = nc.Block()
        self.block = self.block_cm.__enter__()

        pg = Prog(nc)
        self.pg = pg
        self.E = pg.emit
        self.s_x = pg.stream("x")
        self.s_w = pg.stream("w")
        self.s_b = pg.stream("bounce")
        self.s_cc = pg.stream("cc")
        self.s_gi = pg.stream("gin")
        self.s_y = pg.stream("y")
        self.s_wp = [pg.stream("wp0"), pg.stream("wp1")]
        self.s_wg = pg.stream("wg")
        self.s_wg2 = pg.stream("wg2")
        self.s_wg3 = pg.stream("wg3")
        self.s_wg4 = pg.stream("wg4")
        self.s_ws = pg.stream("ws")
        self.s_cb = [pg.stream("cb0"), pg.stream("cb1")]
        self.s_fd = [pg.stream("fd0"), pg.stream("fd1")]
        self.s_gig = pg.stream("gig")
        self.s_vb = pg.stream("vb")
        self.s_sp = pg.stream("spill")
        self.s_rp = [pg.stream("rp0"), pg.stream("rp1")]
        self.s_f = [pg.stream("f0"), pg.stream("f1")]
        self.vb_row = 0

    def _sb(self, name, shape, dt):
        cm = self.nc.sbuf_tensor(name, shape, dt)
        t = cm.__enter__()
        self._guards.append(cm)
        return t

    def sb_scope(self, name, shape, dt):
        cm = self.nc.sbuf_tensor(name, shape, dt)
        return cm, cm.__enter__()

    def ps_scope(self, name, shape):
        cm = self.nc.psum_tensor(name, shape, F32)
        return cm, cm.__enter__()

    def vbounce(self, ncols):
        r = self.vb_row % 16
        self.vb_row += 1
        return r

    def prelude(self):
        E, pg = self.E, self.pg
        self.t_xload = E("sync", lambda e: e.dma_start(
            out=self.x_res[:, :].rearrange("p (c d) -> p c d", c=TC),
            in_=self.x_in.rearrange("(c p) d -> p c d", p=128)),
            dma=True, sem=self.s_x)
        t0 = E("gpsimd", lambda e: e.memset(self.ident_bf[:, :], 0.0))
        t1 = E("gpsimd", lambda e: e.affine_select(
            out=self.ident_bf[:, :], in_=self.ident_bf[:, :],
            compare_op=ALU.not_equal, fill=1.0, base=0,
            pattern=[[-1, 128]], channel_multiplier=1), waits=[t0])
        t2 = E("gpsimd", lambda e: e.memset(self.ident_f8[:, :], 0.0), waits=[t1])
        t3 = E("gpsimd", lambda e: e.affine_select(
            out=self.ident_f8[:, :], in_=self.ident_f8[:, :],
            compare_op=ALU.not_equal, fill=1.0, base=0,
            pattern=[[-1, 128]], channel_multiplier=1), waits=[t2])
        t4 = E("gpsimd", lambda e: e.memset(self.ident_f32[:, :], 0.0), waits=[t3])
        t5 = E("gpsimd", lambda e: e.affine_select(
            out=self.ident_f32[:, :], in_=self.ident_f32[:, :],
            compare_op=ALU.not_equal, fill=1.0, base=0,
            pattern=[[-1, 128]], channel_multiplier=1), waits=[t4])
        self.t_id, self.t_id8, self.t_id32 = t1, t3, t5
        self.t_ones = E("vector", lambda e: e.memset(self.ones_col_bf[:, :], 1.0))
        tz = E("vector", lambda e: e.memset(self.stats[:, :], 0.0))
        tz2 = E("vector", lambda e: e.memset(self.svec[:, :], 0.0), waits=[tz])
        self.t_sv0 = tz2
        E("vector", lambda e: e.memset(self.tiny[:, :], 0.0), waits=[tz2])
        self.t_cidx = E("sync", lambda e: e.dma_start(
            out=self.cidx[:, :], in_=self.cidx_in[:, :]), dma=True, sem=self.s_w)
        if self.n_sub >= 1:
            self._rope_tables()

    def _rope_tables(self):
        E, nc = self.E, self.nc
        g0a, cs_tab = self.sb_scope("rp_cs", [128, S], F32)
        g0b, sn_tab = self.sb_scope("rp_sn", [128, S], F32)
        g1, ang = self.sb_scope("rp_ang", [128, S], F32)
        g2, t1 = self.sb_scope("rp_t1", [128, S], F32)
        g3, i1 = self.sb_scope("rp_i1", [128, S], I32)
        g4, pidx = self.sb_scope("rp_pidx", [128, 1], I32)
        g5, pf = self.sb_scope("rp_pf", [128, 1], F32)
        tA = E("gpsimd", lambda e: e.iota(
            ang[:, :], pattern=[[1, S]], base=0, channel_multiplier=0,
            allow_small_or_imprecise_dtypes=True), waits=[self.t_id8])
        tB = E("gpsimd", lambda e: e.iota(
            pidx[:, :], pattern=[[0, 1]], base=0, channel_multiplier=1), waits=[tA])
        tC = E("vector", lambda e: e.tensor_single_scalar(
            out=pidx[:, :], in_=pidx[:, :], scalar=31, op=ALU.bitwise_and), waits=[tB])
        tD = E("vector", lambda e: e.tensor_copy(out=pf[:, :], in_=pidx[:, :]),
               waits=[tC])
        tE = E("scalar", lambda e: e.activation(
            out=pf[:, :], in_=pf[:, :], func=AF.Exp, scale=-LN1E4 / 32.0), waits=[tD])
        tF = E("vector", lambda e: e.tensor_scalar(
            out=ang[:, :], in0=ang[:, :], scalar1=pf[:, :], scalar2=None,
            op0=ALU.mult), waits=[tE, tA])

        def sin_of(dst, extra, w0):
            tg = E("vector", lambda e: e.tensor_scalar(
                out=t1[:, :], in0=ang[:, :], scalar1=1.0 / TWO_PI,
                scalar2=0.5 + extra / TWO_PI, op0=ALU.mult, op1=ALU.add),
                waits=[w0])
            th = E("vector", lambda e: e.tensor_copy(out=i1[:, :], in_=t1[:, :]),
                   waits=[tg])
            ti = E("vector", lambda e: e.tensor_copy(out=t1[:, :], in_=i1[:, :]),
                   waits=[th])
            tj = E("vector", lambda e: e.scalar_tensor_tensor(
                out=dst[:, :], in0=t1[:, :], scalar=-TWO_PI, in1=ang[:, :],
                op0=ALU.mult, op1=ALU.add), waits=[ti, tF])
            tk = E("vector", lambda e: e.tensor_scalar(
                out=dst[:, :], in0=dst[:, :], scalar1=extra, scalar2=None,
                op0=ALU.add), waits=[tj]) if extra != 0.0 else tj
            tl = E("vector", lambda e: e.tensor_single_scalar(
                out=t1[:, :], in_=dst[:, :], scalar=math.pi, op=ALU.is_gt),
                waits=[tk])
            tm = E("vector", lambda e: e.scalar_tensor_tensor(
                out=dst[:, :], in0=t1[:, :], scalar=-TWO_PI, in1=dst[:, :],
                op0=ALU.mult, op1=ALU.add), waits=[tl])
            tn = E("vector", lambda e: e.tensor_single_scalar(
                out=t1[:, :], in_=dst[:, :], scalar=-math.pi, op=ALU.is_lt),
                waits=[tm])
            to = E("vector", lambda e: e.scalar_tensor_tensor(
                out=dst[:, :], in0=t1[:, :], scalar=TWO_PI, in1=dst[:, :],
                op0=ALU.mult, op1=ALU.add), waits=[tn])
            tp = E("vector", lambda e: e.tensor_scalar(
                out=dst[:, :], in0=dst[:, :], scalar1=-math.pi, scalar2=math.pi,
                op0=ALU.max, op1=ALU.min), waits=[to])
            return E("scalar", lambda e: e.activation(
                out=dst[:, :], in_=dst[:, :], func=AF.Sin), waits=[tp])

        ts_ = sin_of(sn_tab, 0.0, tF)
        tc_ = sin_of(cs_tab, math.pi / 2.0, ts_)
        E("sync", lambda e: e.dma_start(
            out=self.rope_dram.ap()[0], in_=cs_tab[:, :]), dma=True,
            sem=self.s_b, waits=[tc_])
        E("sync", lambda e: e.dma_start(
            out=self.rope_dram.ap()[1], in_=sn_tab[:, :]), dma=True,
            sem=self.s_b, waits=[ts_])
        self.t_rope = self.pg.total(self.s_b)
        for g in (g5, g4, g3, g2, g1, g0b, g0a):
            g.__exit__(None, None, None)

    def ffn_weights(self):
        E, pg, nc = self.E, self.pg, self.nc
        g1, raw = self.sb_scope("qw_raw", [128, IF], F32)
        g2, q8 = self.sb_scope("qw_q8", [128, IF], FP8)
        g3, ws = self.sb_scope("qw_ws", [128, 8], F32)
        g4, tr8 = self.sb_scope("qw_tr", [128, 512], FP8)
        gp, ptr = self.ps_scope("qw_ptr", [128, 512])

        state = {"last_store": None, "last_q": None, "last_ws": None}

        def shard(src, n_pc, n_free, binT, ws_bin):
            last_store = state["last_store"]
            last_q = state["last_q"]
            last_ws = state["last_ws"]
            for pc in range(n_pc):
                tl = E("gpsimd", lambda e, pc=pc: e.dma_start(
                    out=raw[:, 0:n_free], in_=src[pc * 128:(pc + 1) * 128, :]),
                    dma=True, sem=self.s_wg4,
                    waits=[last_store, last_q, getattr(self, "t_rope", None)])
                tlt = pg.total(self.s_wg4)
                tm = E("vector", lambda e: e.tensor_reduce(
                    ws[:, 0:1], raw[:, 0:n_free], axis=AX, op=ALU.max,
                    apply_absolute_value=True), waits=[tlt])
                tm2 = E("vector", lambda e: e.tensor_scalar(
                    out=ws[:, 1:2], in0=ws[:, 0:1], scalar1=1e-30, scalar2=None,
                    op0=ALU.max), waits=[tm])
                tr = E("vector", lambda e: e.reciprocal(ws[:, 2:3], ws[:, 1:2]),
                       waits=[tm2])
                tq = E("vector", lambda e: e.tensor_scalar(
                    out=q8[:, 0:n_free], in0=raw[:, 0:n_free], scalar1=ws[:, 2:3],
                    scalar2=224.0, op0=ALU.mult, op1=ALU.mult), waits=[tr])
                last_q = tq
                twx = E("vector", lambda e: e.tensor_scalar(
                    out=ws[:, 3:4], in0=ws[:, 0:1], scalar1=1.0 / 448.0,
                    scalar2=None, op0=ALU.mult), waits=[tq, last_ws])
                last_ws = E("sync", lambda e, pc=pc: e.dma_start(
                    out=ws_bin.ap()[pc * 128:(pc + 1) * 128, :], in_=ws[:, 3:4]),
                    dma=True, sem=self.s_ws, waits=[twx])
                nfc = n_free // 128
                for fc0 in range(0, nfc, 4):
                    nb = min(4, nfc - fc0)
                    tts = None
                    for j in range(nb):
                        fc = fc0 + j
                        tts = E("tensor", lambda e, fc=fc, j=j: e.matmul(
                            ptr[:, j * 128:(j + 1) * 128],
                            q8[:, fc * 128:(fc + 1) * 128], self.ident_f8[:, :],
                            start=True, stop=True),
                            waits=[tq, self.t_id8, last_store])
                    tev = E("scalar", lambda e, nb=nb: e.copy(
                        out=tr8[:, 0:nb * 128], in_=ptr[:, 0:nb * 128]), waits=[tts])
                    last_store = E("sync", lambda e, pc=pc, fc0=fc0, nb=nb: e.dma_start(
                        out=binT.ap()[fc0 * 128:(fc0 + nb) * 128,
                                      pc * 128:(pc + 1) * 128].rearrange(
                                      "(n p) q -> p n q", p=128),
                        in_=tr8[:, 0:nb * 128].rearrange("p (n q) -> p n q", q=128)),
                        dma=True, sem=self.s_b, waits=[tev])

            state.update(last_store=last_store, last_q=last_q, last_ws=last_ws)

        shard(self.wg_s, ISH // 128, D, self.g_bin, self.wsg_bin)
        shard(self.wu_s, ISH // 128, D, self.u_bin, self.wsu_bin)
        shard(self.wd_s, HSH // 128, IF, self.d_bin, self.wsd_bin)
        t_all = pg.total(self.s_b)
        t_all_ws = pg.total(self.s_ws)
        for (i_, o_) in ((self.g_bin, self.g_bout), (self.u_bin, self.u_bout),
                         (self.d_bin, self.d_bout), (self.wsg_bin, self.wsg_bout),
                         (self.wsu_bin, self.wsu_bout), (self.wsd_bin, self.wsd_bout)):
            E("gpsimd", lambda e, i_=i_, o_=o_: e.collective_compute(
                "AllGather", ALU.bypass, replica_groups=self.ALL8,
                ins=[i_.ap().opt()], outs=[o_.ap().opt()]),
                waits=[t_all, t_all_ws], sem=self.s_cc)
        self.t_ffnw = pg.total(self.s_cc)
        for g in (gp, g4, g3, g2, g1):
            g.__exit__(None, None, None)

    def ln_stats(self, kind, tag):
        E = self.E
        ticks = []
        for c in range(TC):
            xc = self.x_res[:, c * D:(c + 1) * D]
            mu = self.stats[:, c:c + 1]
            rho = self.stats[:, 8 + c:9 + c]
            s1 = self.svec[:, 24:25]
            s2 = self.svec[:, 25:26]
            w = self.svec[:, 26:27]
            y = self.svec[:, 27:28]
            r2 = self.svec[:, 28:29]
            rv = self.svec[:, 29:30]
            prev = ticks[-1] if ticks else self.t_sv0
            ta = E("vector", lambda e, xc=xc, s1=s1: e.reduce_sum(
                s1, xc, axis=AX), waits=[self.t_xload, prev])
            tb = None
            for q4 in range(4):
                tb = E("scalar", lambda e, xc=xc, q4=q4: e.activation(
                    out=self.scr512[:, :], in_=xc[:, q4 * 512:(q4 + 1) * 512],
                    func=AF.Square, accum_out=self.svec[:, q4:q4 + 1]),
                    waits=[self.t_xload, prev, tb])
            ts_ = E("vector", lambda e, s2=s2: e.reduce_sum(
                s2, self.svec[:, 0:4], axis=AX), waits=[tb])
            tm2 = E("vector", lambda e, mu=mu, s1=s1: e.tensor_scalar(
                out=mu, in0=s1, scalar1=1.0 / D, scalar2=None, op0=ALU.mult),
                waits=[ta])
            td = E("vector", lambda e, w=w, mu=mu: e.tensor_tensor(
                out=w, in0=mu, in1=mu, op=ALU.mult), waits=[tm2])
            te = E("vector", lambda e, w=w, s2=s2: e.scalar_tensor_tensor(
                out=w, in0=s2, scalar=1.0 / D, in1=w, op0=ALU.mult,
                op1=ALU.subtract), waits=[td, ts_])
            tw3 = E("vector", lambda e, w=w: e.tensor_scalar(
                out=w, in0=w, scalar1=EPS, scalar2=None, op0=ALU.add), waits=[te])
            tg = E("scalar", lambda e, y=y, w=w: e.activation(
                out=y, in_=w, func=AF.Ln), waits=[tw3, tm2])
            th = E("scalar", lambda e, y=y: e.activation(
                out=y, in_=y, func=AF.Exp, scale=-0.5), waits=[tg])
            ti = E("vector", lambda e, r2=r2, y=y: e.tensor_tensor(
                out=r2, in0=y, in1=y, op=ALU.mult), waits=[th])
            tj = E("vector", lambda e, r2=r2, w=w: e.tensor_tensor(
                out=r2, in0=r2, in1=w, op=ALU.mult), waits=[ti])
            tk = E("vector", lambda e, r2=r2: e.tensor_scalar(
                out=r2, in0=r2, scalar1=-0.5, scalar2=1.5, op0=ALU.mult,
                op1=ALU.add), waits=[tj])
            tl = E("vector", lambda e, y=y, r2=r2: e.tensor_tensor(
                out=y, in0=y, in1=r2, op=ALU.mult), waits=[tk])
            if kind == "a":
                yp = self.stats[:, 16 + c:17 + c]
                tl = E("vector", lambda e, yp=yp, y=y: e.tensor_copy(
                    out=yp, in_=y), waits=[tl])
                tm_ = E("vector", lambda e, rv=rv, w=w: e.reciprocal(rv, w),
                        waits=[tw3, tl])
                tn = E("vector", lambda e, rv=rv: e.tensor_scalar(
                    out=rv, in0=rv, scalar1=-EPS, scalar2=1.0 + EPS,
                    op0=ALU.mult, op1=ALU.add), waits=[tm_])
                to = E("scalar", lambda e, rv=rv: e.activation(
                    out=rv, in_=rv, func=AF.Ln), waits=[tn])
                tp = E("scalar", lambda e, rv=rv: e.activation(
                    out=rv, in_=rv, func=AF.Exp, scale=-0.5), waits=[to])
                tq = E("vector", lambda e, rho=rho, y=y, rv=rv: e.tensor_tensor(
                    out=rho, in0=y, in1=rv, op=ALU.mult), waits=[tp, tl])
            else:
                tq = E("vector", lambda e, rho=rho, y=y: e.tensor_copy(
                    out=rho, in_=y), waits=[tl])
            ticks.append(tq)
        return ticks

    def attn(self, kind):
        E, pg, nc = self.E, self.pg, self.nc
        la = kind == "a"
        wsrc = self.win_s if la else self.wqkvT_s
        wo_src = self.wout_a if la else self.woutT_b

        t_ln = self.ln_stats("a" if la else "p", f"at{kind}")

        if la:
            gw, lnwB = self.sb_scope(f"lnwB{kind}", [128, D], F32)
            gb, lnbB = self.sb_scope(f"lnbB{kind}", [128, D], F32)
            E("sync", lambda e: e.dma_start(
                out=lnwB[:, :], in_=self.lnw_in[0:1, :].partition_broadcast(128)),
                dma=True, sem=self.s_w)
            E("sync", lambda e: e.dma_start(
                out=lnbB[:, :], in_=self.lnb_in[0:1, :].partition_broadcast(128)),
                dma=True, sem=self.s_w)
            t_lnt = pg.total(self.s_w)
        else:
            lnwB = lnbB = None
            t_lnt = None

        t_lnbv_bc = t_lnbq = None
        lnbVb = lnbQK = None
        gvb = gqk = None
        if la:
            gvb, lnbVb = self.sb_scope(f"a{kind}_lnbVb", [128, HM * DH], F32)
            gqk, lnbQK = self.sb_scope(f"a{kind}_lnbQK", [128, 16], F32)
            glb, lnb_dm = self.sb_scope("lnb_dm", [128, DC], BF16)
            glr, lnb_row = self.sb_scope("lnb_row", [1, 3 * HM * DH], F32)
            gwt, w_t = self.sb_scope("lnb_wt", [128, DC * 128], BF16)
            gpq, p_ln = self.ps_scope("p_lnb", [1, 512])
            tlb = E("gpsimd", lambda e: e.dma_start(
                out=lnb_dm[:, :],
                in_=self.lnb_in.rearrange("a (dc p) -> a p dc", p=128)[0],
                allow_slow_non_contiguous=True),
                dma=True, sem=self.s_wg)
            tlbt = pg.total(self.s_wg)
            ev = None
            for f in range(24):
                tl2 = E("gpsimd", lambda e, f=f: e.dma_start(
                    out=w_t[:, :].rearrange("p (dc q) -> p dc q", dc=DC), in_=wsrc[:, f * 128:(f + 1) * 128].rearrange(
                        "(dc p) q -> p dc q", p=128)), dma=True, sem=self.s_wg2,
                    waits=[ev])
                tl2t = pg.total(self.s_wg2)
                mm = None
                for dc in range(DC):
                    mm = E("tensor", lambda e, f=f, dc=dc: e.matmul(
                        p_ln[0:1, 0:128], lnb_dm[:, dc:dc + 1],
                        w_t[:, dc * 128:(dc + 1) * 128],
                        start=(dc == 0), stop=(dc == DC - 1)),
                        waits=[tlbt, tl2t, ev])
                ev = E("vector", lambda e, f=f: e.tensor_copy(
                    out=lnb_row[0:1, f * 128:(f + 1) * 128], in_=p_ln[0:1, 0:128]),
                    waits=[mm])
            tst = E("sync", lambda e: e.dma_start(
                out=self.vb.ap()[0:1, 0:3 * HM * DH], in_=lnb_row[0:1, :]),
                dma=True, sem=self.s_vb, waits=[ev])
            _t1 = E("sync", lambda e: e.dma_start(
                out=lnbQK[:, :],
                in_=self.vb.ap()[0:1, 0:2048].rearrange("a (f p) -> a p f", p=128)[0],
                allow_slow_non_contiguous=True),
                dma=True, sem=self.s_vb, waits=[tst])
            _t2 = E("sync", lambda e: e.dma_start(
                out=lnbVb[:, :],
                in_=self.vb.ap()[0:1, 2048:2048 + HM * DH].partition_broadcast(128)),
                dma=True, sem=self.s_vb, waits=[tst])
            t_lnbq = t_lnbv_bc = pg.total(self.s_vb)
            gpq.__exit__(None, None, None)
            gwt.__exit__(None, None, None)
            glr.__exit__(None, None, None)
            glb.__exit__(None, None, None)

        gxn, xn_tm = self.sb_scope(f"a{kind}_xn", [128, TC * D], BF16)
        t_xn = []
        for c in range(TC):
            xc = self.x_res[:, c * D:(c + 1) * D]
            mu = self.stats[:, c:c + 1]
            rho = self.stats[:, 8 + c:9 + c]
            dst = xn_tm[:, c * D:(c + 1) * D]
            if la:
                tv = E("vector", lambda e, xc=xc, mu=mu, rho=rho: e.tensor_scalar(
                    out=self.scrD[:, :], in0=xc, scalar1=mu, scalar2=rho,
                    op0=ALU.subtract, op1=ALU.mult),
                    waits=[t_ln[c], t_ln[-1], t_xn[-1] if t_xn else None])
                tw = E("vector", lambda e, dst=dst: e.tensor_tensor(
                    out=dst, in0=self.scrD[:, :], in1=lnwB[:, :], op=ALU.mult),
                    waits=[tv, t_lnt, t_lnbq])
                t_xn.append(tw)
            else:
                tv = E("vector", lambda e, xc=xc, mu=mu, rho=rho, dst=dst:
                       e.tensor_scalar(out=dst, in0=xc, scalar1=mu, scalar2=rho,
                                       op0=ALU.subtract, op1=ALU.mult),
                       waits=[t_ln[c], t_ln[-1]])
                t_xn.append(tv)

        gxm, xnT_m = self.sb_scope(f"a{kind}_xnTm", [128, DC * T], BF16)
        gpt, ptr = self.ps_scope(f"a{kind}_ptr", [128, 512])
        ev_prev = None
        for dc in range(DC):
            for t4 in range(0, TC, 4):
                tt = None
                for j in range(4):
                    c = t4 + j
                    tt = E("tensor", lambda e, c=c, dc=dc, j=j: e.matmul(
                        ptr[:, j * 128:(j + 1) * 128],
                        xn_tm[:, c * D + dc * 128:c * D + (dc + 1) * 128],
                        self.ident_bf[:, :], start=True, stop=True),
                        waits=[t_xn[c], self.t_id, ev_prev])
                ev_prev = E("scalar", lambda e, dc=dc, t4=t4: e.copy(
                    out=xnT_m[:, dc * T + t4 * 128:dc * T + (t4 + 4) * 128],
                    in_=ptr[:, :]), waits=[tt])
        t_xn_bout = E("sync", lambda e: e.dma_start(
            out=self.xn_bin.ap().rearrange("d p t -> p d t"),
            in_=xnT_m[:, :].rearrange("p (d t) -> p d t", d=DC)),
            dma=True, sem=self.s_b, waits=[ev_prev])
        t_ag_xn = E("gpsimd", lambda e: e.collective_compute(
            "AllGather", ALU.bypass, replica_groups=self.PAIRS,
            ins=[self.xn_bin.ap().opt()], outs=[self.xn_bout.ap().opt()]),
            waits=[pg.total(self.s_b)], sem=self.s_cc)
        t_spill = E("scalar", lambda e: e.dma_start(
            out=self.x_spill, in_=self.x_res[:, :]), dma=True, sem=self.s_sp,
            waits=[t_ln[-1], t_xn[-1]])
        gpt.__exit__(None, None, None)
        gxm.__exit__(None, None, None)
        gxn.__exit__(None, None, None)

        qg = self.x_res[:, 0:TC * D // 2].bitcast(BF16)
        kg = self.x_res[:, TC * D // 2:TC * D].bitcast(BF16)

        gv, v_tm = self.sb_scope(f"a{kind}_v", [128, KC * HM * DH], BF16)
        gxT, xnT = self.sb_scope(f"a{kind}_xnT", [128, DC * S], BF16)
        for sl in range(2):
            E("sync", lambda e, sl=sl: e.dma_start(
                out=xnT[:, :].rearrange("p (d h t) -> p d h t", d=DC, h=2)[:, :, sl, :],
                in_=self.xn_bout.ap()[sl].rearrange("d p t -> p d t")),
                dma=True, sem=self.s_gi, waits=[t_ag_xn])
        t_xnT = pg.total(self.s_gi)

        gwv, wv = self.sb_scope(f"a{kind}_wv", [128, DC * HM * DH // 2], BF16)
        gv0, pv0 = self.ps_scope(f"a{kind}_pv0", [128, 512])
        gv1, pv1 = self.ps_scope(f"a{kind}_pv1", [128, 512])
        ev_pv = [None, None]
        t_wv_t = None
        for fh in range(2):
            t_wv = E("gpsimd", lambda e, fh=fh: e.dma_start(
                out=wv[:, :].rearrange("p (dc f) -> p dc f", dc=DC),
                in_=wsrc[:, 2 * HM * DH + fh * 512:2 * HM * DH + (fh + 1) * 512]
                .rearrange("(dc p) f -> p dc f", p=128)),
                dma=True, sem=self.s_wg3,
                waits=[ev_pv[0], ev_pv[1]])
            t_wv_t = pg.total(self.s_wg3)
            for kc in range(KC):
                pv = pv0 if kc % 2 == 0 else pv1
                mm = None
                for dc in range(DC):
                    mm = E("tensor", lambda e, kc=kc, dc=dc, pv=pv, fh=fh: e.matmul(
                        pv[:, :],
                        xnT[:, dc * S + kc * 128:dc * S + (kc + 1) * 128],
                        wv[:, dc * 512:(dc + 1) * 512],
                        start=(dc == 0), stop=(dc == DC - 1)),
                        waits=[t_xnT, t_wv_t, ev_pv[kc % 2]])
                dst = v_tm[:, kc * HM * DH + fh * 512:kc * HM * DH + (fh + 1) * 512]
                if la:
                    ev_pv[kc % 2] = E("vector", lambda e, pv=pv, dst=dst, fh=fh:
                                      e.tensor_tensor(
                        out=dst, in0=pv[:, :],
                        in1=lnbVb[:, fh * 512:(fh + 1) * 512], op=ALU.add),
                        waits=[mm, t_lnbv_bc])
                else:
                    ev_pv[kc % 2] = E("vector", lambda e, pv=pv, dst=dst:
                                      e.tensor_copy(out=dst, in_=pv[:, :]),
                                      waits=[mm])
                t_v = pg.total("vector")
        gv1.__exit__(None, None, None)
        gv0.__exit__(None, None, None)
        gwv.__exit__(None, None, None)

        gw0, wqk0 = self.sb_scope(f"a{kind}_wqk0", [128, DC * 128], BF16)
        gw1, wqk1 = self.sb_scope(f"a{kind}_wqk1", [128, DC * 128], BF16)
        gq0, pq0 = self.ps_scope(f"a{kind}_pq0", [128, 2048])
        gq1, pq1 = self.ps_scope(f"a{kind}_pq1", [128, 2048])
        wbufs = [wqk0, wqk1]
        pbufs = [pq0, pq1]
        ev_qk = [None, None]
        t_qk = []
        for ft in range(16):
            par = ft % 2
            wb = wbufs[par]
            E("gpsimd", lambda e, ft=ft, wb=wb: e.dma_start(
                out=wb[:, :].rearrange("p (dc f) -> p dc f", dc=DC),
                in_=wsrc[:, ft * 128:(ft + 1) * 128].rearrange(
                    "(dc p) f -> p dc f", p=128)),
                dma=True, sem=self.s_wp[par],
                waits=[ev_qk[par], t_v if ft < 2 else None])
            twt = pg.total(self.s_wp[par])
            pq = pbufs[par]
            mm = None
            for sl2 in range(4):
                for dc in range(DC):
                    mm = E("tensor", lambda e, sl2=sl2, dc=dc, pq=pq, wb=wb: e.matmul(
                        pq[:, sl2 * 512:(sl2 + 1) * 512],
                        wb[:, dc * 128:(dc + 1) * 128],
                        xnT[:, dc * S + sl2 * 512:dc * S + (sl2 + 1) * 512],
                        start=(dc == 0), stop=(dc == DC - 1)),
                        waits=[twt, t_xnT, ev_qk[par], t_v if ft < 2 else None])
            dst_t = qg if ft < 8 else kg
            dst = dst_t[:, (ft % 8) * S:(ft % 8 + 1) * S]
            if la:
                ev_qk[par] = E("vector", lambda e, pq=pq, dst=dst, ft=ft:
                               e.tensor_scalar(
                    out=dst, in0=pq[:, :], scalar1=lnbQK[:, ft:ft + 1],
                    scalar2=None, op0=ALU.add), waits=[mm, t_lnbq, t_spill])
            else:
                ev_qk[par] = E("scalar", lambda e, pq=pq, dst=dst: e.copy(
                    out=dst, in_=pq[:, :]), waits=[mm, t_spill])
            t_qk.append(ev_qk[par])
        gq1.__exit__(None, None, None)
        gq0.__exit__(None, None, None)
        gw1.__exit__(None, None, None)
        gw0.__exit__(None, None, None)
        gxT.__exit__(None, None, None)

        if os.environ.get("MK_DEBUG") and la:
            E("gpsimd", lambda e: e.dma_start(
                out=self.dbg.ap()[5], in_=qg[:, 4 * S:5 * S]), dma=True,
                sem=self.s_wg, waits=[t_qk[4], t_qk[-1]])
            E("gpsimd", lambda e: e.dma_start(
                out=self.dbg.ap()[6], in_=xnT[:, 0:S]), dma=True,
                sem=self.s_wg, waits=[t_xnT, t_qk[-1]])
        t_rope_done = list(t_qk)
        rope_guards = []
        if la:
            gm1, m1 = self.sb_scope("rp_m1", [128, S], F32)
            gm2, m2 = self.sb_scope("rp_m2", [128, S], F32)
            gct, cs_t = self.sb_scope("rp_cst", [128, S], F32)
            gst, sn_t = self.sb_scope("rp_snt", [128, S], F32)
            rope_guards = [gst, gct, gm2, gm1]
            E("sync", lambda e: e.dma_start(
                out=cs_t[:, :], in_=self.rope_dram.ap()[0]), dma=True,
                sem=self.s_gi, waits=[self.t_rope, t_xnT, t_qk[-1],
                                      pg.total(self.s_wg)])
            E("sync", lambda e: e.dma_start(
                out=sn_t[:, :], in_=self.rope_dram.ap()[1]), dma=True,
                sem=self.s_gi, waits=[self.t_rope, t_xnT, t_qk[-1],
                                      pg.total(self.s_wg)])
            t_ropeload = pg.total(self.s_gi)
            last = None
            for tgt, base in ((qg, t_qk[0:8]), (kg, t_qk[8:16])):
                for eo in range(2):
                    et = tgt[:, eo * S:(eo + 1) * S]
                    ot = tgt[:, (2 + eo) * S:(3 + eo) * S]
                    w0 = [base[eo], base[2 + eo], t_ropeload, last]
                    ta = E("vector", lambda e, et=et: e.tensor_tensor(
                        out=m1[:, :], in0=et, in1=cs_t[:, :], op=ALU.mult), waits=w0)
                    tb = E("vector", lambda e, ot=ot: e.tensor_tensor(
                        out=m2[:, :], in0=ot, in1=sn_t[:, :], op=ALU.mult),
                        waits=[ta])
                    tc_ = E("vector", lambda e: e.tensor_sub(
                        out=m2[:, :], in0=m1[:, :], in1=m2[:, :]), waits=[tb])
                    td = E("vector", lambda e, ot=ot: e.tensor_tensor(
                        out=m1[:, :], in0=ot, in1=cs_t[:, :], op=ALU.mult),
                        waits=[tc_])
                    te = E("vector", lambda e, et=et, ot=ot: e.tensor_tensor(
                        out=ot, in0=et, in1=sn_t[:, :], op=ALU.mult), waits=[td])
                    tf = E("vector", lambda e, ot=ot: e.tensor_tensor(
                        out=ot, in0=ot, in1=m1[:, :], op=ALU.add), waits=[te])
                    tg = E("vector", lambda e, et=et: e.tensor_copy(
                        out=et, in_=m2[:, :]), waits=[tf])
                    last = tg
            t_rope_done = [last] + t_qk

        gq2a, q2a = self.sb_scope(f"a{kind}_q2a", [128, S], BF16)
        gq2b, q2b = self.sb_scope(f"a{kind}_q2b", [128, S], BF16)
        gk2a, k2a = self.sb_scope(f"a{kind}_k2a", [128, S], BF16)
        gk2b, k2b = self.sb_scope(f"a{kind}_k2b", [128, S], BF16)
        gpt0, psA = self.ps_scope(f"a{kind}_psA", [128, 1024])
        gpt1, psB = self.ps_scope(f"a{kind}_psB", [128, 1024])
        gpc, psC = self.ps_scope(f"a{kind}_psC", [128, 1024])
        gpz, psZ = self.ps_scope(f"a{kind}_psZ", [1, 1024])
        gsq, sqb = self.sb_scope(f"a{kind}_sq", [128, S], BF16)
        gpp0, pt0 = self.sb_scope(f"a{kind}_pt0", [128, 1024], BF16)
        gpp1, pt1 = self.sb_scope(f"a{kind}_pt1", [128, 1024], BF16)
        gzr, zrow = self.sb_scope(f"a{kind}_zrow", [1, 1024], F32)
        gzb, zb = self.sb_scope(f"a{kind}_zb", [128, 1024], F32)
        gbb, biasb = self.sb_scope(f"a{kind}_bias", [128, 2], F32)
        gce0, ctxe0 = self.sb_scope(f"a{kind}_ce0", [128, 1024], BF16)
        gce1, ctxe1 = self.sb_scope(f"a{kind}_ce1", [128, 1024], BF16)
        q2s, k2s = [q2a, q2b], [k2a, k2b]
        ctxes = [ctxe0, ctxe1]
        t_rep_free = [None, None]
        t_bias_free = [None, None]
        t_ce_free = [None, None]
        t_ctx_ev = []
        t_zrow_free = None
        t_zb_free = None
        t_sq_free = None
        t_psz_free = None
        prev_exp = [None, None]
        prev_pt_use = [None, None]
        t_qk_all = t_qk[-1]
        ce_i = 0
        for h in range(HM):
            par = h % 2
            if la:
                q2, k2 = q2s[par][:, :], k2s[par][:, :]
                segs = [(0, 32, h // 4, 32 * (h % 4)),
                        (32, 64, 2 + h // 4, 32 * (h % 4)),
                        (64, 128, 4 + h // 2, 64 * (h % 2))]
                for (d0, d1, tile, p0) in segs:
                    for dst_t, src_t in ((q2, qg), (k2, kg)):
                        E("sync", lambda e, d0=d0, d1=d1, tile=tile, p0=p0,
                          dst_t=dst_t, src_t=src_t: e.dma_start(
                            out=dst_t[d0:d1, :],
                            in_=src_t[p0:p0 + (d1 - d0), tile * S:(tile + 1) * S]),
                            dma=True, sem=self.s_rp[par],
                            waits=[t_rope_done[0], t_rep_free[par]])
                t_rp = pg.total(self.s_rp[par])
                t_rp2 = t_rp
            else:
                q2 = qg[:, h * S:(h + 1) * S]
                k2 = kg[:, h * S:(h + 1) * S]
                t_rp = t_qk[h]
                t_rp2 = t_qk[8 + h]
            for side, src_ in ((0, q2), (1, k2)):
                tsq = E("vector", lambda e, src_=src_: e.tensor_tensor(
                    out=sqb[:, :], in0=src_, in1=src_, op=ALU.mult),
                    waits=[t_rp, t_rp2, t_sq_free,
                           t_qk_all if h == 0 else None])
                for half in range(2):
                    mm = None
                    for sl2 in range(2):
                        mm = E("tensor", lambda e, half=half, sl2=sl2: e.matmul(
                            psZ[0:1, sl2 * 512:(sl2 + 1) * 512],
                            self.ones_col_bf[:, :],
                            sqb[:, half * 1024 + sl2 * 512:
                                half * 1024 + (sl2 + 1) * 512],
                            start=True, stop=True),
                            waits=[tsq, self.t_ones, t_psz_free,
                                   t_qk_all if h == 0 else None])
                    t_sq_free = mm
                    col = self.tiny[0:1, 8 + side * 2 + half:9 + side * 2 + half]
                    t_psz_free = E("vector", lambda e, col=col: e.tensor_reduce(
                        col, psZ[0:1, :], axis=AX, op=ALU.max), waits=[mm])
            tmx = E("vector", lambda e: e.tensor_reduce(
                self.tiny[0:1, 12:13], self.tiny[0:1, 8:10], axis=AX, op=ALU.max),
                waits=[t_psz_free])
            tmx2 = E("vector", lambda e: e.tensor_reduce(
                self.tiny[0:1, 13:14], self.tiny[0:1, 10:12], axis=AX, op=ALU.max),
                waits=[tmx])
            tpd = E("vector", lambda e: e.tensor_tensor(
                out=self.tiny[0:1, 14:15], in0=self.tiny[0:1, 12:13],
                in1=self.tiny[0:1, 13:14], op=ALU.mult), waits=[tmx2])
            tsq2 = E("scalar", lambda e: e.activation(
                out=self.tiny[0:1, 15:16], in_=self.tiny[0:1, 14:15], func=AF.Sqrt,
                scale=(RSQ * 1.01) ** 2), waits=[tpd])
            tng = E("vector", lambda e: e.tensor_scalar(
                out=self.tiny[0:1, 16:17], in0=self.tiny[0:1, 15:16], scalar1=-1.0,
                scalar2=None, op0=ALU.mult), waits=[tsq2])
            tvb = E("sync", lambda e: e.dma_start(
                out=self.vb.ap()[2:3, 0:1], in_=self.tiny[0:1, 16:17]),
                dma=True, sem=self.s_vb, waits=[tng, pg.total(self.s_vb)])
            t_bias = E("sync", lambda e, par=par: e.dma_start(
                out=biasb[:, par:par + 1],
                in_=self.vb.ap()[2:3, 0:1].partition_broadcast(128)),
                dma=True, sem=self.s_vb, waits=[tvb, t_bias_free[par]])
            if os.environ.get("MK_DEBUG") and h == 0 and la:
                E("gpsimd", lambda e, q2=q2: e.dma_start(
                    out=self.dbg.ap()[0], in_=q2), dma=True, sem=self.s_wg,
                    waits=[t_rp, t_rp2])
                E("gpsimd", lambda e, k2=k2: e.dma_start(
                    out=self.dbg.ap()[1], in_=k2), dma=True, sem=self.s_wg,
                    waits=[t_rp, t_rp2])
            for qh in range(2):
                mm_z = None
                mm_c = None
                for kc in range(KC):
                    kpar = kc % 2
                    psS = psA if kpar == 0 else psB
                    ptb = pt0 if kpar == 0 else pt1
                    mm_s = None
                    for sl2 in range(2):
                        mm_s = E("tensor", lambda e, kc=kc, qh=qh, sl2=sl2, psS=psS,
                                 q2=q2, k2=k2: e.matmul(
                            psS[:, sl2 * 512:(sl2 + 1) * 512],
                            k2[:, kc * 128:(kc + 1) * 128],
                            q2[:, qh * 1024 + sl2 * 512:qh * 1024 + (sl2 + 1) * 512],
                            start=True, stop=True),
                            waits=[t_rp, t_rp2, prev_exp[kpar], t_bias])
                    t_exp = E("scalar", lambda e, psS=psS, ptb=ptb, par=par:
                              e.activation(
                        out=ptb[:, :], in_=psS[:, :], func=AF.Exp, scale=RSQ,
                        bias=biasb[:, par:par + 1]),
                        waits=[mm_s, t_bias, prev_pt_use[kpar]])
                    prev_exp[kpar] = t_exp
                    t_bias_free[par] = t_exp
                    if os.environ.get("MK_DEBUG") and h == 0 and qh == 0 \
                            and kc == 0 and la:
                        t_exp = E("gpsimd", lambda e, ptb=ptb: e.dma_start(
                            out=self.dbg.ap()[3][:, 0:1024], in_=ptb[:, :]),
                            dma=True, sem=self.s_wg, waits=[t_exp])
                        prev_exp[kpar] = t_exp
                    for sl2 in range(2):
                        mm_z = E("tensor", lambda e, kc=kc, sl2=sl2, ptb=ptb: e.matmul(
                            psZ[0:1, sl2 * 512:(sl2 + 1) * 512],
                            self.ones_col_bf[:, :],
                            ptb[:, sl2 * 512:(sl2 + 1) * 512],
                            start=(kc == 0), stop=(kc == KC - 1)),
                            waits=[t_exp, self.t_ones,
                                   t_psz_free if kc == 0 else None])
                    for sl2 in range(2):
                        mm_c = E("tensor", lambda e, kc=kc, sl2=sl2, h=h, ptb=ptb:
                                 e.matmul(
                            psC[:, sl2 * 512:(sl2 + 1) * 512],
                            v_tm[:, kc * HM * DH + h * DH:kc * HM * DH + (h + 1) * DH],
                            ptb[:, sl2 * 512:(sl2 + 1) * 512],
                            start=(kc == 0), stop=(kc == KC - 1)),
                            waits=[t_exp, t_v,
                                   t_ctx_ev[-1] if (kc == 0 and t_ctx_ev) else None])
                    prev_pt_use[kpar] = mm_c
                tze = E("vector", lambda e: e.tensor_copy(
                    out=zrow[0:1, :], in_=psZ[0:1, :]), waits=[mm_z, t_zrow_free])
                t_psz_free = tze
                if os.environ.get("MK_DEBUG") and h == 0 and qh == 0 and la:
                    tze = E("gpsimd", lambda e: e.dma_start(
                        out=self.dbg.ap()[4][0:1, 0:1024], in_=zrow[0:1, :]),
                        dma=True, sem=self.s_wg, waits=[tze])
                trc = E("vector", lambda e: e.reciprocal(zrow[0:1, :], zrow[0:1, :]),
                        waits=[tze])
                tzs = E("sync", lambda e: e.dma_start(
                    out=self.vb.ap()[3:4, 0:1024], in_=zrow[0:1, :]),
                    dma=True, sem=self.s_vb, waits=[trc, pg.total(self.s_vb)])
                t_zrow_free = tzs
                tzb = E("sync", lambda e: e.dma_start(
                    out=zb[:, :],
                    in_=self.vb.ap()[3:4, 0:1024].partition_broadcast(128)),
                    dma=True, sem=self.s_vb, waits=[tzs, t_zb_free])
                cpar = ce_i % 2
                ce = ctxes[cpar]
                tce = E("vector", lambda e, ce=ce: e.tensor_tensor(
                    out=ce[:, :], in0=psC[:, :], in1=zb[:, :], op=ALU.mult),
                    waits=[mm_c, tzb, t_ce_free[cpar]])
                t_zb_free = tce
                tcd = E("sync", lambda e, h=h, qh=qh, ce=ce: e.dma_start(
                    out=self.ctx_bin.ap()[h][:, qh * 1024:(qh + 1) * 1024],
                    in_=ce[:, :]), dma=True, sem=self.s_cb[cpar], waits=[tce])
                t_ce_free[cpar] = tcd
                ce_i += 1
                t_ctx_ev.append(tce)
            t_rep_free[par] = t_ctx_ev[-1]
        t_ctx_out = pg.total(self.s_cb[0])
        t_ctx_out2 = pg.total(self.s_cb[1])
        t_attn_done = t_ctx_ev[-1]
        for g in (gce1, gce0, gbb, gzb, gzr, gpp1, gpp0, gsq, gpz, gpc,
                  gpt1, gpt0, gk2b, gk2a, gq2b, gq2a):
            g.__exit__(None, None, None)
        for g in rope_guards:
            g.__exit__(None, None, None)
        gv.__exit__(None, None, None)
        if la:
            gqk.__exit__(None, None, None)
            gvb.__exit__(None, None, None)

        t_ag_c = E("gpsimd", lambda e: e.collective_compute(
            "AllGather", ALU.bypass, replica_groups=self.PAIRS,
            ins=[self.ctx_bin.ap().opt()], outs=[self.ctx_bout.ap().opt()]),
            waits=[t_ctx_out, t_ctx_out2], sem=self.s_cc)

        gca, ctx_all = self.sb_scope(f"a{kind}_ctxall", [128, NH * T], BF16)
        for gh in range(NH):
            E("gpsimd", lambda e, gh=gh: e.indirect_dma_start(
                out=ctx_all[:, gh * T:(gh + 1) * T], out_offset=None,
                in_=self.ctx_bout.ap(),
                in_offset=IndirectOffsetOnAxis(ap=self.cidx[:, gh:gh + 1], axis=0)),
                dma=True, sem=self.s_gig, waits=[t_ag_c, pg.total(self.s_w)])
        t_ctxall = pg.total(self.s_gig)

        t_xre = E("scalar", lambda e: e.dma_start(
            out=self.x_res[:, :], in_=self.x_spill), dma=True, sem=self.s_sp,
            waits=[pg.total(self.s_sp), t_attn_done, t_rope_done[0],
                   pg.total(self.s_rp[0]), pg.total(self.s_rp[1])])
        t_xrel = pg.total(self.s_sp)
        gwo0, wo0 = self.sb_scope(f"a{kind}_wo0", [128, D], BF16)
        gwo1, wo1 = self.sb_scope(f"a{kind}_wo1", [128, D], BF16)
        gpo0, po0 = self.ps_scope(f"a{kind}_po0", [128, 2048])
        gpo1, po1 = self.ps_scope(f"a{kind}_po1", [128, 2048])
        wos, pos = [wo0, wo1], [po0, po1]
        ev_o = [None, None]
        t_wuse = [None, None]
        for rnd in range(TC // 2):
            last_mm = [None, None]
            for dc in range(DC):
                wpar = dc % 2
                wob = wos[wpar]
                E("gpsimd", lambda e, dc=dc, wob=wob: e.dma_start(
                    out=wob[:, :], in_=wo_src[dc * 128:(dc + 1) * 128, :]),
                    dma=True, sem=self.s_wp[wpar], waits=[t_wuse[wpar]])
                twt = pg.total(self.s_wp[wpar])
                for tpar in range(2):
                    tc2 = rnd * 2 + tpar
                    po = pos[tpar]
                    for sl2 in range(4):
                        last_mm[tpar] = E("tensor",
                            lambda e, dc=dc, tc2=tc2, sl2=sl2, po=po, wob=wob:
                            e.matmul(
                                po[:, sl2 * 512:(sl2 + 1) * 512],
                                ctx_all[:, dc * T + tc2 * 128:dc * T + (tc2 + 1) * 128],
                                wob[:, sl2 * 512:(sl2 + 1) * 512],
                                start=(dc == 0), stop=(dc == DC - 1)),
                            waits=[twt, t_ctxall, ev_o[tpar],
                                   t_attn_done if (rnd == 0 and dc == 0) else None])
                t_wuse[wpar] = last_mm[1]
            for tpar in range(2):
                tc2 = rnd * 2 + tpar
                po = pos[tpar]
                xc = self.x_res[:, tc2 * D:(tc2 + 1) * D]
                mu = self.stats[:, tc2:tc2 + 1]
                rho = self.stats[:, 8 + tc2:9 + tc2]
                if la:
                    yp = self.stats[:, 16 + tc2:17 + tc2]
                    tv1 = E("vector", lambda e, xc=xc, mu=mu, yp=yp: e.tensor_scalar(
                        out=self.scrD[:, :], in0=xc, scalar1=mu, scalar2=yp,
                        op0=ALU.subtract, op1=ALU.mult),
                        waits=[last_mm[tpar], t_xrel, ev_o[0], ev_o[1]])
                    tv4 = E("vector", lambda e, po=po: e.tensor_tensor(
                        out=self.scrD[:, :], in0=self.scrD[:, :], in1=po[:, :],
                        op=ALU.add), waits=[tv1])
                    ev_o[tpar] = E("vector", lambda e, xc=xc: e.tensor_tensor(
                        out=xc, in0=xc, in1=self.scrD[:, :], op=ALU.add), waits=[tv4])
                else:
                    ev_o[tpar] = E("vector", lambda e, xc=xc, po=po: e.tensor_tensor(
                        out=xc, in0=xc, in1=po[:, :], op=ALU.add),
                        waits=[last_mm[tpar], t_xrel])
        for g in (gpo1, gpo0, gwo1, gwo0, gca):
            g.__exit__(None, None, None)
        if la:
            gb.__exit__(None, None, None)
            gw.__exit__(None, None, None)
        self.pg.barrier()

    def ffn(self, idx):
        E, pg, nc = self.E, self.pg, self.nc
        t_ln = self.ln_stats("p", f"f{idx}")

        gwsg, wsg = self.sb_scope(f"f{idx}_wsg", [128, IPAD // 128], F32)
        gwsu, wsu = self.sb_scope(f"f{idx}_wsu", [128, IPAD // 128], F32)
        gwsh, wshB = self.sb_scope(f"f{idx}_wshB", [128, D], F32)
        E("sync", lambda e: e.dma_start(
            out=wsg[:, :], in_=self.wsg_bout.ap().rearrange(
                "(c p) a -> a p c", p=128)[0],
            allow_slow_non_contiguous=True), dma=True, sem=self.s_w,
            waits=[self.t_ffnw])
        E("sync", lambda e: e.dma_start(
            out=wsu[:, :], in_=self.wsu_bout.ap().rearrange(
                "(c p) a -> a p c", p=128)[0],
            allow_slow_non_contiguous=True), dma=True, sem=self.s_w,
            waits=[self.t_ffnw])
        E("sync", lambda e: e.dma_start(
            out=wshB[:, :], in_=self.wsd_bout.ap().rearrange(
                "d c -> c d").partition_broadcast(128)),
            dma=True, sem=self.s_w, waits=[self.t_ffnw])
        t_ws = pg.total(self.s_w)

        gqxT, qxT = self.sb_scope(f"f{idx}_qxT", [128, DC * G], FP8)
        gtb, t_i = self.sb_scope(f"f{idx}_t", [128, IC * G], BF16)
        gqt, qt_i = self.sb_scope(f"f{idx}_qt", [128, IC * G], FP8)
        gqx, qx_tm = self.sb_scope(f"f{idx}_qx", [128, GC * D], FP8)
        gsA, Atile = self.sb_scope(f"f{idx}_At", [128, G], F32)
        gsBp, Bptile = self.sb_scope(f"f{idx}_Bp", [128, G], F32)
        gmac, macc = self.sb_scope(f"f{idx}_macc", [128, G], F32)
        gmac2, macc2 = self.sb_scope(f"f{idx}_macc2", [128, G], F32)
        ggs0, gsl0 = self.sb_scope(f"f{idx}_gs0", [128, DC * 128], FP8)
        ggs1, gsl1 = self.sb_scope(f"f{idx}_gs1", [128, DC * 128], FP8)
        gus0, usl0 = self.sb_scope(f"f{idx}_us0", [128, DC * 128], FP8)
        gus1, usl1 = self.sb_scope(f"f{idx}_us1", [128, DC * 128], FP8)
        gds0, dsl0 = self.sb_scope(f"f{idx}_ds0", [128, D], FP8)
        gds1, dsl1 = self.sb_scope(f"f{idx}_ds1", [128, D], FP8)
        gscr, scrG = self.sb_scope(f"f{idx}_scrG", [128, 3 * 512], F32)
        gpg0, psg0 = self.ps_scope(f"f{idx}_psg0", [128, 512])
        gpg1, psg1 = self.ps_scope(f"f{idx}_psg1", [128, 512])
        gpu0, psu0 = self.ps_scope(f"f{idx}_psu0", [128, 512])
        gpu1, psu1 = self.ps_scope(f"f{idx}_psu1", [128, 512])
        gpd, psD = self.ps_scope(f"f{idx}_psD", [128, 2048])
        ptr = psD

        t_group_done = None
        t_down_mm_done = None
        for g in range(FG):
            t_qx = []
            t_acol = []
            for j in range(GC):
                c = g * GC + j
                xc = self.x_res[:, c * D:(c + 1) * D]
                mu = self.stats[:, c:c + 1]
                rho = self.stats[:, 8 + c:9 + c]
                mx = self.stats[:, 24:25]
                sc_ = self.stats[:, 25:26]
                acol = self.stats[:, 16 + j:17 + j]
                tv1 = E("vector", lambda e, xc=xc, mu=mu: e.tensor_scalar(
                    out=self.scrD[:, :], in0=xc, scalar1=mu, scalar2=None,
                    op0=ALU.subtract), waits=[t_ln[c], t_group_done,
                                              t_qx[-1] if t_qx else None])
                tv2 = E("vector", lambda e, mx=mx: e.tensor_reduce(
                    mx, self.scrD[:, :], axis=AX, op=ALU.max,
                    apply_absolute_value=True), waits=[tv1])
                tv2b = E("vector", lambda e, mx=mx: e.tensor_scalar(
                    out=mx, in0=mx, scalar1=1e-30, scalar2=None, op0=ALU.max),
                    waits=[tv2])
                tv3 = E("vector", lambda e, sc_=sc_, mx=mx: e.reciprocal(sc_, mx),
                        waits=[tv2b])
                tv4 = E("vector", lambda e, j=j, sc_=sc_: e.tensor_scalar(
                    out=qx_tm[:, j * D:(j + 1) * D], in0=self.scrD[:, :],
                    scalar1=sc_, scalar2=224.0, op0=ALU.mult, op1=ALU.mult),
                    waits=[tv3])
                ta = E("vector", lambda e, acol=acol, mx=mx, rho=rho: e.tensor_scalar(
                    out=acol, in0=mx, scalar1=rho, scalar2=4.0 / 448.0,
                    op0=ALU.mult, op1=ALU.mult), waits=[tv3, tv4])
                t_qx.append(tv4)
                t_acol.append(ta)
            for j in range(GC):
                E("sync", lambda e, j=j, g=g: e.dma_start(
                    out=self.vb.ap()[5 + g:6 + g, j * 128:(j + 1) * 128],
                    in_=self.stats[:, 16 + j:17 + j]), dma=True, sem=self.s_vb,
                    waits=[t_acol[j], pg.total(self.s_vb)])
            t_At = E("sync", lambda e, g=g: e.dma_start(
                out=Atile[:, :],
                in_=self.vb.ap()[5 + g:6 + g, 0:G].partition_broadcast(128)),
                dma=True, sem=self.s_vb, waits=[pg.total(self.s_vb)])
            ev_tr = None
            for dc in range(DC):
                tt = None
                for j in range(GC):
                    tt = E("tensor", lambda e, j=j, dc=dc: e.matmul(
                        ptr[:, j * 128:(j + 1) * 128],
                        qx_tm[:, j * D + dc * 128:j * D + (dc + 1) * 128],
                        self.ident_f8[:, :], start=True, stop=True),
                        waits=[t_qx[j], self.t_id8, ev_tr, t_group_done])
                ev_tr = E("scalar", lambda e, dc=dc: e.copy(
                    out=qxT[:, dc * G:(dc + 1) * G], in_=ptr[:, 0:G]), waits=[tt])
            t_qxT = ev_tr

            gslabs, uslabs = [gsl0, gsl1], [usl0, usl1]
            psgs, psus = [psg0, psg1], [psu0, psu1]
            ev_g = [None, None]
            t_slab_free = [None, None]
            t_tmax = None
            t_ti = []
            for ic in range(IC):
                par = ic % 2
                gslab, uslab = gslabs[par], uslabs[par]
                rk, lc = ic // 6, ic % 6
                E("sync", lambda e, rk=rk, lc=lc, gslab=gslab: e.dma_start(
                    out=gslab[:, :].rearrange("p (hc i) -> p hc i", hc=DC),
                    in_=self.g_bout.ap()[rk, :, lc * 128:(lc + 1) * 128].rearrange(
                        "(hc p) i -> p hc i", p=128)),
                    dma=True, sem=self.s_f[par],
                    waits=[t_slab_free[par], self.t_ffnw])
                E("sync", lambda e, rk=rk, lc=lc, uslab=uslab: e.dma_start(
                    out=uslab[:, :].rearrange("p (hc i) -> p hc i", hc=DC),
                    in_=self.u_bout.ap()[rk, :, lc * 128:(lc + 1) * 128].rearrange(
                        "(hc p) i -> p hc i", p=128)),
                    dma=True, sem=self.s_f[par],
                    waits=[t_slab_free[par], self.t_ffnw])
                t_sl = pg.total(self.s_f[par])
                psg, psu = psgs[par], psus[par]
                mm_g = mm_u = None
                for hc in range(DC):
                    mm_g = E("tensor", lambda e, hc=hc, psg=psg, gslab=gslab: e.matmul(
                        psg[:, :], gslab[:, hc * 128:(hc + 1) * 128],
                        qxT[:, hc * G:(hc + 1) * G],
                        start=(hc == 0), stop=(hc == DC - 1)),
                        waits=[t_sl, t_qxT, ev_g[par]])
                for hc in range(DC):
                    mm_u = E("tensor", lambda e, hc=hc, psu=psu, uslab=uslab: e.matmul(
                        psu[:, :], uslab[:, hc * 128:(hc + 1) * 128],
                        qxT[:, hc * G:(hc + 1) * G],
                        start=(hc == 0), stop=(hc == DC - 1)),
                        waits=[t_sl, t_qxT, ev_g[par]])
                t_slab_free[par] = mm_u
                s_g1 = scrG[:, 0:512]
                s_u1 = scrG[:, 512:1024]
                s_sl = scrG[:, 1024:1536]
                te1 = E("scalar", lambda e, ic=ic, psg=psg, s_g1=s_g1: e.activation(
                    out=s_g1, in_=psg[:, :], func=AF.Copy,
                    scale=wsg[:, ic:ic + 1]),
                    waits=[mm_g, t_ws, t_ti[-1] if t_ti else None])
                te2 = E("vector", lambda e, s_g1=s_g1: e.tensor_tensor(
                    out=s_g1, in0=s_g1, in1=Atile[:, :], op=ALU.mult),
                    waits=[te1, t_At])
                te3 = E("scalar", lambda e, s_g1=s_g1, s_sl=s_sl: e.activation(
                    out=s_sl, in_=s_g1, func=AF.Silu), waits=[te2])
                te4 = E("scalar", lambda e, ic=ic, psu=psu, s_u1=s_u1: e.activation(
                    out=s_u1, in_=psu[:, :], func=AF.Copy,
                    scale=wsu[:, ic:ic + 1]), waits=[mm_u, t_ws, te3])
                te5 = E("vector", lambda e, ic=ic, s_sl=s_sl, s_u1=s_u1:
                        e.tensor_tensor(
                    out=t_i[:, ic * G:(ic + 1) * G], in0=s_sl, in1=s_u1,
                    op=ALU.mult), waits=[te3, te4])
                ev_g[par] = te5
                t_ti.append(te5)
                if ic == 0:
                    ta1 = E("vector", lambda e, ic=ic: e.tensor_copy(
                        out=macc[:, :], in_=t_i[:, ic * G:(ic + 1) * G]),
                        waits=[te5, t_group_done])
                    t_tmax = E("vector", lambda e, ic=ic: e.tensor_copy(
                        out=macc2[:, :], in_=t_i[:, ic * G:(ic + 1) * G]),
                        waits=[te5, ta1])
                else:
                    ta1 = E("vector", lambda e, ic=ic: e.tensor_tensor(
                        out=macc[:, :], in0=macc[:, :],
                        in1=t_i[:, ic * G:(ic + 1) * G], op=ALU.max),
                        waits=[te5, t_tmax])
                    t_tmax = E("vector", lambda e, ic=ic: e.tensor_tensor(
                        out=macc2[:, :], in0=macc2[:, :],
                        in1=t_i[:, ic * G:(ic + 1) * G], op=ALU.min),
                        waits=[te5, ta1])
            tn1 = E("vector", lambda e: e.tensor_scalar(
                out=macc2[:, :], in0=macc2[:, :], scalar1=-1.0, scalar2=None,
                op0=ALU.mult), waits=[t_tmax])
            t_tmax = E("vector", lambda e: e.tensor_tensor(
                out=macc[:, :], in0=macc[:, :], in1=macc2[:, :], op=ALU.max),
                waits=[tn1])
            t_m = []
            tt_prev = None
            for j in range(GC):
                tt = E("tensor", lambda e, j=j: e.matmul(
                    ptr[:, 0:128], macc[:, j * 128:(j + 1) * 128],
                    self.ident_f32[:, :], start=True, stop=True),
                    waits=[t_tmax, self.t_id32,
                           t_m[-1] if t_m else t_qxT, t_down_mm_done])
                col = self.stats[:, 26 + j:27 + j]
                tm_ = E("vector", lambda e, col=col: e.tensor_reduce(
                    col, ptr[:, 0:128], axis=AX, op=ALU.max), waits=[tt])
                tm2 = E("vector", lambda e, col=col: e.tensor_scalar(
                    out=col, in0=col, scalar1=1e-30, scalar2=None, op0=ALU.max),
                    waits=[tm_])
                t_m.append(tm2)
            t_b = []
            last_rs = None
            for j in range(GC):
                col = self.stats[:, 26 + j:27 + j]
                acol = self.stats[:, 16 + j:17 + j]
                bcol = self.stats[:, 20 + j:21 + j]
                tb1 = E("vector", lambda e, bcol=bcol, col=col, acol=acol:
                        e.tensor_scalar(
                    out=bcol, in0=col, scalar1=acol, scalar2=4.0 / 448.0,
                    op0=ALU.mult, op1=ALU.mult), waits=[t_m[j]])
                t_b.append(tb1)
                rcol = self.stats[:, 30:31]
                tr1 = E("vector", lambda e, rcol=rcol, col=col: e.reciprocal(
                    rcol, col), waits=[tb1, last_rs])
                tr2 = E("vector", lambda e, rcol=rcol: e.tensor_scalar(
                    out=rcol, in0=rcol, scalar1=224.0, scalar2=None, op0=ALU.mult),
                    waits=[tr1])
                last_rs = E("sync", lambda e, j=j, g=g, rcol=rcol: e.dma_start(
                    out=self.vb.ap()[7 + g:8 + g, j * 128:(j + 1) * 128], in_=rcol),
                    dma=True, sem=self.s_vb, waits=[tr2, pg.total(self.s_vb)])
            t_Bp = E("sync", lambda e, g=g: e.dma_start(
                out=Bptile[:, :],
                in_=self.vb.ap()[7 + g:8 + g, 0:G].partition_broadcast(128)),
                dma=True, sem=self.s_vb, waits=[pg.total(self.s_vb)])
            t_qt = []
            for ic in range(IC):
                t_qt.append(E("vector", lambda e, ic=ic: e.tensor_tensor(
                    out=qt_i[:, ic * G:(ic + 1) * G], in0=t_i[:, ic * G:(ic + 1) * G],
                    in1=Bptile[:, :], op=ALU.mult), waits=[t_Bp, t_ti[ic]]))
            dslabs = [dsl0, dsl1]
            t_dfree = [None, None]
            ev_d = None
            for jp in range(GC // 2):
                mm_d = [[None] * 4, [None] * 4]
                for ic in range(IC):
                    par = ic % 2
                    dslab = dslabs[par]
                    E("sync", lambda e, ic=ic, dslab=dslab: e.dma_start(
                        out=dslab[:, :].rearrange("p (r q) -> p r q", r=NCORE),
                        in_=self.d_bout.ap()[:, ic * 128:(ic + 1) * 128, :]
                        .rearrange("r p q -> p r q")),
                        dma=True, sem=self.s_fd[par],
                        waits=[t_dfree[par], self.t_ffnw])
                    t_dl = pg.total(self.s_fd[par])
                    for jj in range(2):
                        j = jp * 2 + jj
                        for sl2 in range(4):
                            pd_sl = (psD[:, sl2 * 512:(sl2 + 1) * 512] if jj == 0
                                     else [psg0, psg1, psu0, psu1][sl2][:, :])
                            mm_d[jj][sl2] = E("tensor",
                                lambda e, ic=ic, j=j, sl2=sl2, dslab=dslab,
                                pd_sl=pd_sl: e.matmul(
                                    pd_sl,
                                    qt_i[:, ic * G + j * 128:ic * G + (j + 1) * 128],
                                    dslab[:, sl2 * 512:(sl2 + 1) * 512],
                                    start=(ic == 0), stop=(ic == IC - 1)),
                                waits=[t_dl, t_qt[ic], ev_d,
                                       (t_m[-1] if (jp == 0 and ic == 0) else None),
                                       (t_ti[-1] if (jp == 0 and ic == 0) else None)])
                    t_dfree[par] = mm_d[1][3]
                t_down_mm_done = mm_d[1][3]
                for jj in range(2):
                    j = jp * 2 + jj
                    c = g * GC + j
                    bcol = self.stats[:, 20 + j:21 + j]
                    for sl2 in range(4):
                        pd_sl = (psD[:, sl2 * 512:(sl2 + 1) * 512] if jj == 0
                                 else [psg0, psg1, psu0, psu1][sl2][:, :])
                        xs = self.x_res[:, c * D + sl2 * 512:c * D + (sl2 + 1) * 512]
                        td1 = E("vector", lambda e, bcol=bcol, pd_sl=pd_sl, sl2=sl2:
                                e.scalar_tensor_tensor(
                            out=self.scrD[:, sl2 * 512:(sl2 + 1) * 512],
                            in0=pd_sl, scalar=bcol,
                            in1=wshB[:, sl2 * 512:(sl2 + 1) * 512],
                            op0=ALU.mult, op1=ALU.mult),
                            waits=[mm_d[jj][sl2], t_b[j], t_ws, ev_d])
                        ev_d = E("vector", lambda e, xs=xs, sl2=sl2: e.tensor_tensor(
                            out=xs, in0=xs,
                            in1=self.scrD[:, sl2 * 512:(sl2 + 1) * 512], op=ALU.add),
                            waits=[td1])
                    t_group_done = ev_d
        for gg in (gpd, gpu1, gpu0, gpg1, gpg0, gscr, gds1, gds0, gus1,
                   gus0, ggs1, ggs0, gmac2, gmac, gsBp, gsA, gqx, gqt, gtb, gqxT,
                   gwsh, gwsu, gwsg):
            gg.__exit__(None, None, None)
        self.pg.barrier()

    def finale(self):
        E, pg = self.E, self.pg
        if os.environ.get("MK_DEBUG") or os.environ.get("MK_DEBUG2"):
            waits_all = [pg.total("vector"), pg.total("scalar"), pg.total("tensor"),
                         pg.total(self.s_cc), pg.total(self.s_gig), pg.total(self.s_wg)]
            t1 = E("sync", lambda e: e.dma_start(
                out=self.dbg_out, in_=self.dbg.ap()), dma=True, sem=self.s_y,
                waits=waits_all)
            t2 = E("sync", lambda e: e.dma_start(
                out=self.xn_dbg, in_=self.xn_bout.ap()), dma=True, sem=self.s_y,
                waits=waits_all)
            t3 = E("sync", lambda e: e.dma_start(
                out=self.ctx_dbg, in_=self.ctx_bout.ap()), dma=True, sem=self.s_y,
                waits=waits_all)
        t_y = E("sync", lambda e: e.dma_start(
            out=self.y_out.rearrange("(c p) d -> p c d", p=128),
            in_=self.x_res[:, :].rearrange("p (c d) -> p c d", c=TC)),
            dma=True, sem=self.s_y,
            waits=[pg.total("vector"), pg.total("scalar"), pg.total("tensor")])
        pg.wait_only("sync", [t_y])
        pg.wait_only("gpsimd", [t_y])

    def finish(self):
        self.pg.install(self.block)
        self.block_cm.__exit__(None, None, None)
        for g in reversed(self._guards):
            g.__exit__(None, None, None)
        return self.nc


def build_program(n_sub=4):
    mk = MK(n_sub)
    mk.prelude()
    if n_sub >= 2:
        mk.ffn_weights()
    mk.pg.barrier()
    if n_sub >= 1:
        mk.attn("a")
    if n_sub >= 2:
        mk.ffn(0)
    if n_sub >= 3:
        mk.attn("b")
    if n_sub >= 4:
        mk.ffn(1)
    mk.finale()
    return mk.finish()



def _shard_inputs(inputs, core):
    b, half = core // 2, core % 2
    heads = [8 * half + j for j in range(HM)]
    W_in = inputs["L0_attn_a_W_in"]
    def qk_cols(base):
        cols = []
        for et in range(2):
            for hh in range(4):
                h = heads[et * 4 + hh]
                cols.extend(base + DH * h + np.arange(0, 64, 2))
        for ot in range(2):
            for hh in range(4):
                h = heads[ot * 4 + hh]
                cols.extend(base + DH * h + np.arange(1, 64, 2))
        for pt in range(4):
            for hh in range(2):
                h = heads[pt * 2 + hh]
                cols.extend(base + DH * h + np.arange(64, 128))
        return cols
    v_cols = []
    for h in heads:
        v_cols.extend(2 * D + DH * h + np.arange(DH))
    win_cols = np.array(qk_cols(0) + qk_cols(D) + v_cols)
    win_s = np.ascontiguousarray(W_in[:, win_cols])

    w_qkv = inputs["L0_attn_b_w_qkv"]
    rows = []
    for base in (0, NH * DH, 2 * NH * DH):
        for h in heads:
            rows.extend(base + DH * h + np.arange(DH))
    wqkvT_s = np.ascontiguousarray(w_qkv[np.array(rows), :].T)

    wg = inputs["L0_ffn_w_gate"]
    wu = inputs["L0_ffn_w_up"]
    wd = inputs["L0_ffn_w_down"]
    down_idx = np.asarray(inputs["L0_ffn_down_reorder_index"])
    inv = np.argsort(down_idx)
    wdp = np.ascontiguousarray(wd[:, inv])

    def pad_shard(w):
        out = np.zeros((ISH, D), dtype=np.float32)
        lo, hi = core * ISH, min((core + 1) * ISH, IF)
        if hi > lo:
            out[0:hi - lo] = w[lo:hi]
        return out

    x = np.ascontiguousarray(inputs["L0_attn_a_X"][b, half * T:(half + 1) * T, :])

    cidx = np.zeros((128, NH), dtype=np.int32)
    p = np.arange(128)
    for gh in range(NH):
        slot, hh = gh // HM, gh % HM
        cidx[:, gh] = ((slot * HM + hh) * 128 + p) * 2 + half

    return {
        "x_in": x,
        "win_s": win_s.astype(np.float32),
        "lnw_in": np.ascontiguousarray(inputs["L0_attn_a_ln_w"]).reshape(1, D).astype(np.float32),
        "lnb_in": np.ascontiguousarray(inputs["L0_attn_a_ln_b"]).reshape(1, D).astype(np.float32),
        "wout_a": np.ascontiguousarray(inputs["L0_attn_a_W_out"]).astype(np.float32),
        "wqkvT_s": wqkvT_s.astype(np.float32),
        "woutT_b": np.ascontiguousarray(inputs["L0_attn_b_w_out"].T).astype(np.float32),
        "wg_s": pad_shard(wg).astype(np.float32),
        "wu_s": pad_shard(wu).astype(np.float32),
        "wd_s": np.ascontiguousarray(
            wdp[core * HSH:(core + 1) * HSH]).astype(np.float32),
        "cidx_in": cidx,
    }


_cache = {}


def kernel(**inputs):
    n_sub = int(os.environ.get("MK_NSUB", "4"))
    nc = build_program(n_sub)
    in_maps = [_shard_inputs(inputs, c) for c in range(NCORE)]
    res = run_bass_kernel_spmd(nc, in_maps, core_ids=list(range(NCORE)))
    out = np.zeros((B, S, D), dtype=np.float32)
    for c in range(NCORE):
        b, half = c // 2, c % 2
        out[b, half * T:(half + 1) * T, :] = res.results[c]["y_out"]
    return out


# version 06 (final)
# speedup vs baseline: 1.0161x; 1.0161x over previous
import math
import os

import numpy as np
import concourse.bass as bass
import concourse.mybir as mybir
from concourse.bass import IndirectOffsetOnAxis
from concourse.bass_utils import run_bass_kernel_spmd

F32 = mybir.dt.float32
BF16 = mybir.dt.bfloat16
FP8 = mybir.dt.float8e4
I32 = mybir.dt.int32
AF = mybir.ActivationFunctionType
ALU = mybir.AluOpType
AX = mybir.AxisListType.X

NCORE = 8
B, S, D = 4, 2048, 2048
NH, DH = 16, 128
HM = 8
T = 1024
TC = T // 128
DC = D // 128
KC = S // 128
IF = 5632
IC = IF // 128
IPAD = 6144
ISH = IPAD // NCORE
HSH = D // NCORE
FG = 2
G = T // FG
GC = G // 128
EPS = 1e-5
RSQ = 1.0 / math.sqrt(128.0)
LN1E4 = math.log(10000.0)
TWO_PI = 2.0 * math.pi

ENGINES = ("sync", "gpsimd", "vector", "scalar", "tensor")


class Prog:

    def __init__(self, nc):
        self.nc = nc
        self.ops = {e: [] for e in ENGINES}
        self.count = {e: 0 for e in ENGINES}
        self.sems = {}
        self._cms = []
        self.waited = {e: set() for e in ENGINES}
        for e in ENGINES:
            cm = nc.semaphore(f"pg_{e}")
            self.sems[e] = cm.__enter__()
            self._cms.append(cm)

    def stream(self, name):
        cm = self.nc.semaphore(f"st_{name}")
        self.sems[name] = cm.__enter__()
        self._cms.append(cm)
        self.count[name] = 0
        return name

    def emit(self, eng, fn, dma=False, waits=(), sem=None):
        key = sem if sem is not None else eng
        step = 16 if dma else 1
        self.count[key] += step
        tick = (key, self.count[key])
        waits = tuple(w for w in waits if w is not None)
        for (wk, wv) in waits:
            if wk in ENGINES:
                self.waited[wk].add(wv)
        self.ops[eng].append((fn, key, step, self.count[key], waits))
        return tick

    def wait_only(self, eng, waits):
        waits = tuple(w for w in waits if w is not None)
        for (wk, wv) in waits:
            if wk in ENGINES:
                self.waited[wk].add(wv)
        self.ops[eng].append((None, None, 0, 0, waits))

    def total(self, key):
        return (key, self.count[key])

    def barrier(self, exclude=()):
        totals = [(k, v) for k, v in self.count.items()
                  if v > 0 and k not in exclude]
        for e in ENGINES:
            self.wait_only(e, [t for t in totals if t[0] != e])

    def install(self, block):
        remap = {}
        for e in ENGINES:
            vals = sorted(self.waited[e])
            remap[e] = {v: i + 1 for i, v in enumerate(vals)}
        sems = self.sems

        def map_wait(wk, wv):
            if wk in ENGINES:
                vals = remap[wk]
                if wv in vals:
                    return vals[wv]
                bigger = [nv for ov, nv in vals.items() if ov >= wv]
                return min(bigger) if bigger else len(vals)
            return wv

        for e in ENGINES:
            ops = self.ops[e]
            rm = remap[e]

            def make(ops, rm, e):
                def body(engine):
                    for (fn, key, step, tick, waits) in ops:
                        for (wk, wv) in waits:
                            engine.wait_ge(sems[wk], map_wait(wk, wv))
                        if fn is None:
                            continue
                        inst = fn(engine)
                        if inst is None:
                            continue
                        if key in ENGINES:
                            if tick in self.waited[key]:
                                inst.then_inc(sems[key], 1)
                        else:
                            inst.then_inc(sems[key], step)
                return body
            getattr(block, e)(make(ops, rm, e))


class MK:
    def __init__(self, n_sub=4):
        self.n_sub = n_sub
        nc = bass.Bass("TRN2", target_bir_lowering=False, debug=False)
        self.nc = nc
        self._guards = []

        dr = lambda *a, **k: nc.dram_tensor(*a, **k).ap()
        self.x_in = dr("x_in", [T, D], F32, kind="ExternalInput")
        self.y_out = dr("y_out", [T, D], F32, kind="ExternalOutput")
        self.win_s = dr("win_s", [D, 3 * HM * DH], F32, kind="ExternalInput")
        self.lnw_in = dr("lnw_in", [1, D], F32, kind="ExternalInput")
        self.lnb_in = dr("lnb_in", [1, D], F32, kind="ExternalInput")
        self.wout_a = dr("wout_a", [D, D], F32, kind="ExternalInput")
        self.wqkvT_s = dr("wqkvT_s", [D, 3 * HM * DH], F32, kind="ExternalInput")
        self.woutT_b = dr("woutT_b", [D, D], F32, kind="ExternalInput")
        self.wg_s = dr("wg_s", [ISH, D], F32, kind="ExternalInput")
        self.wu_s = dr("wu_s", [ISH, D], F32, kind="ExternalInput")
        self.wd_s = dr("wd_s", [HSH, IF], F32, kind="ExternalInput")
        self.cidx_in = dr("cidx_in", [128, NH], I32, kind="ExternalInput")

        self.x_spill = dr("x_spill", [128, TC * D], F32)
        self.xn_bin = nc.dram_tensor("xn_bin", [DC, 128, T], BF16)
        self.xn_bout = nc.dram_tensor("xn_bout", [2, DC, 128, T], BF16)
        self.ctx_bin = nc.dram_tensor("ctx_bin", [HM, 128, S], BF16)
        self.ctx_bout = nc.dram_tensor("ctx_bout", [2 * HM * 128 * 2, T], BF16)
        self.g_bin = nc.dram_tensor("g_bin", [D, ISH], FP8)
        self.g_bout = nc.dram_tensor("g_bout", [NCORE, D, ISH], FP8)
        self.u_bin = nc.dram_tensor("u_bin", [D, ISH], FP8)
        self.u_bout = nc.dram_tensor("u_bout", [NCORE, D, ISH], FP8)
        self.d_bin = nc.dram_tensor("d_bin", [IF, HSH], FP8)
        self.d_bout = nc.dram_tensor("d_bout", [NCORE, IF, HSH], FP8)
        self.wsg_bin = nc.dram_tensor("wsg_bin", [ISH, 1], F32)
        self.wsg_bout = nc.dram_tensor("wsg_bout", [IPAD, 1], F32)
        self.wsu_bin = nc.dram_tensor("wsu_bin", [ISH, 1], F32)
        self.wsu_bout = nc.dram_tensor("wsu_bout", [IPAD, 1], F32)
        self.wsd_bin = nc.dram_tensor("wsd_bin", [HSH, 1], F32)
        self.wsd_bout = nc.dram_tensor("wsd_bout", [D, 1], F32)
        self.vb = nc.dram_tensor("vb", [16, 2 * S], F32)
        self.rope_dram = nc.dram_tensor("rope_dram", [2, 128, S], F32)
        self.dbg = nc.dram_tensor("dbg", [8, 128, S], F32)
        if os.environ.get("MK_DEBUG") or os.environ.get("MK_DEBUG2"):
            self.dbg_out = dr("dbg_out", [8, 128, S], F32, kind="ExternalOutput")
            self.xn_dbg = dr("xn_dbg", [2, DC, 128, T], BF16, kind="ExternalOutput")
            self.ctx_dbg = dr("ctx_dbg", [2 * HM * 128 * 2, T], BF16,
                              kind="ExternalOutput")

        self.PAIRS = [[0, 1], [2, 3], [4, 5], [6, 7]]
        self.ALL8 = [list(range(NCORE))]

        self.x_res = self._sb("x_res", [128, TC * D], F32)
        self.ident_bf = self._sb("ident_bf", [128, 128], BF16)
        self.ident_f8 = self._sb("ident_f8", [128, 128], FP8)
        self.ident_f32 = self._sb("ident_f32", [128, 128], F32)
        self.ones_col_bf = self._sb("ones_col_bf", [128, 1], BF16)
        self.stats = self._sb("stats", [128, 40], F32)
        self.svec = self._sb("svec", [128, 32], F32)
        self.tiny = self._sb("tiny", [1, 64], F32)
        self.cidx = self._sb("cidx", [128, NH], I32)
        self.scrD = self._sb("scrD", [128, D], F32)
        self.scr512 = self._sb("scr512", [128, 512], F32)

        self.block_cm = nc.Block()
        self.block = self.block_cm.__enter__()

        pg = Prog(nc)
        self.pg = pg
        self.E = pg.emit
        self.s_x = pg.stream("x")
        self.s_w = pg.stream("w")
        self.s_b = pg.stream("bounce")
        self.s_cc = pg.stream("cc")
        self.s_gi = pg.stream("gin")
        self.s_y = pg.stream("y")
        self.s_wp = [pg.stream("wp0"), pg.stream("wp1")]
        self.s_wg = pg.stream("wg")
        self.s_wg2 = pg.stream("wg2")
        self.s_wg3 = pg.stream("wg3")
        self.s_wg4 = pg.stream("wg4")
        self.s_ws = pg.stream("ws")
        self.s_cb = [pg.stream("cb0"), pg.stream("cb1")]
        self.s_fd = [pg.stream("fd0"), pg.stream("fd1")]
        self.s_gig = pg.stream("gig")
        self.s_vb = pg.stream("vb")
        self.s_sp = pg.stream("spill")
        self.s_rp = [pg.stream("rp0"), pg.stream("rp1")]
        self.s_f = [pg.stream("f0"), pg.stream("f1")]
        self.vb_row = 0

    def _sb(self, name, shape, dt):
        cm = self.nc.sbuf_tensor(name, shape, dt)
        t = cm.__enter__()
        self._guards.append(cm)
        return t

    def sb_scope(self, name, shape, dt):
        cm = self.nc.sbuf_tensor(name, shape, dt)
        return cm, cm.__enter__()

    def ps_scope(self, name, shape):
        cm = self.nc.psum_tensor(name, shape, F32)
        return cm, cm.__enter__()

    def vbounce(self, ncols):
        r = self.vb_row % 16
        self.vb_row += 1
        return r

    def prelude(self):
        E, pg = self.E, self.pg
        self.t_xload = E("sync", lambda e: e.dma_start(
            out=self.x_res[:, :].rearrange("p (c d) -> p c d", c=TC),
            in_=self.x_in.rearrange("(c p) d -> p c d", p=128)),
            dma=True, sem=self.s_x)
        t0 = E("gpsimd", lambda e: e.memset(self.ident_bf[:, :], 0.0))
        t1 = E("gpsimd", lambda e: e.affine_select(
            out=self.ident_bf[:, :], in_=self.ident_bf[:, :],
            compare_op=ALU.not_equal, fill=1.0, base=0,
            pattern=[[-1, 128]], channel_multiplier=1), waits=[t0])
        t2 = E("gpsimd", lambda e: e.memset(self.ident_f8[:, :], 0.0), waits=[t1])
        t3 = E("gpsimd", lambda e: e.affine_select(
            out=self.ident_f8[:, :], in_=self.ident_f8[:, :],
            compare_op=ALU.not_equal, fill=1.0, base=0,
            pattern=[[-1, 128]], channel_multiplier=1), waits=[t2])
        t4 = E("gpsimd", lambda e: e.memset(self.ident_f32[:, :], 0.0), waits=[t3])
        t5 = E("gpsimd", lambda e: e.affine_select(
            out=self.ident_f32[:, :], in_=self.ident_f32[:, :],
            compare_op=ALU.not_equal, fill=1.0, base=0,
            pattern=[[-1, 128]], channel_multiplier=1), waits=[t4])
        self.t_id, self.t_id8, self.t_id32 = t1, t3, t5
        self.t_ones = E("vector", lambda e: e.memset(self.ones_col_bf[:, :], 1.0))
        tz = E("vector", lambda e: e.memset(self.stats[:, :], 0.0))
        tz2 = E("vector", lambda e: e.memset(self.svec[:, :], 0.0), waits=[tz])
        self.t_sv0 = tz2
        E("vector", lambda e: e.memset(self.tiny[:, :], 0.0), waits=[tz2])
        self.t_cidx = E("sync", lambda e: e.dma_start(
            out=self.cidx[:, :], in_=self.cidx_in[:, :]), dma=True, sem=self.s_w)
        if self.n_sub >= 1:
            self._rope_tables()

    def _rope_tables(self):
        E, nc = self.E, self.nc
        g0a, cs_tab = self.sb_scope("rp_cs", [128, S], F32)
        g0b, sn_tab = self.sb_scope("rp_sn", [128, S], F32)
        g1, ang = self.sb_scope("rp_ang", [128, S], F32)
        g2, t1 = self.sb_scope("rp_t1", [128, S], F32)
        g3, i1 = self.sb_scope("rp_i1", [128, S], I32)
        g4, pidx = self.sb_scope("rp_pidx", [128, 1], I32)
        g5, pf = self.sb_scope("rp_pf", [128, 1], F32)
        tA = E("gpsimd", lambda e: e.iota(
            ang[:, :], pattern=[[1, S]], base=0, channel_multiplier=0,
            allow_small_or_imprecise_dtypes=True), waits=[self.t_id8])
        tB = E("gpsimd", lambda e: e.iota(
            pidx[:, :], pattern=[[0, 1]], base=0, channel_multiplier=1), waits=[tA])
        tC = E("vector", lambda e: e.tensor_single_scalar(
            out=pidx[:, :], in_=pidx[:, :], scalar=31, op=ALU.bitwise_and), waits=[tB])
        tD = E("vector", lambda e: e.tensor_copy(out=pf[:, :], in_=pidx[:, :]),
               waits=[tC])
        tE = E("scalar", lambda e: e.activation(
            out=pf[:, :], in_=pf[:, :], func=AF.Exp, scale=-LN1E4 / 32.0), waits=[tD])
        tF = E("vector", lambda e: e.tensor_scalar(
            out=ang[:, :], in0=ang[:, :], scalar1=pf[:, :], scalar2=None,
            op0=ALU.mult), waits=[tE, tA])

        def sin_of(dst, extra, w0):
            tg = E("vector", lambda e: e.tensor_scalar(
                out=t1[:, :], in0=ang[:, :], scalar1=1.0 / TWO_PI,
                scalar2=0.5 + extra / TWO_PI, op0=ALU.mult, op1=ALU.add),
                waits=[w0])
            th = E("vector", lambda e: e.tensor_copy(out=i1[:, :], in_=t1[:, :]),
                   waits=[tg])
            ti = E("vector", lambda e: e.tensor_copy(out=t1[:, :], in_=i1[:, :]),
                   waits=[th])
            tj = E("vector", lambda e: e.scalar_tensor_tensor(
                out=dst[:, :], in0=t1[:, :], scalar=-TWO_PI, in1=ang[:, :],
                op0=ALU.mult, op1=ALU.add), waits=[ti, tF])
            tk = E("vector", lambda e: e.tensor_scalar(
                out=dst[:, :], in0=dst[:, :], scalar1=extra, scalar2=None,
                op0=ALU.add), waits=[tj]) if extra != 0.0 else tj
            tl = E("vector", lambda e: e.tensor_single_scalar(
                out=t1[:, :], in_=dst[:, :], scalar=math.pi, op=ALU.is_gt),
                waits=[tk])
            tm = E("vector", lambda e: e.scalar_tensor_tensor(
                out=dst[:, :], in0=t1[:, :], scalar=-TWO_PI, in1=dst[:, :],
                op0=ALU.mult, op1=ALU.add), waits=[tl])
            tn = E("vector", lambda e: e.tensor_single_scalar(
                out=t1[:, :], in_=dst[:, :], scalar=-math.pi, op=ALU.is_lt),
                waits=[tm])
            to = E("vector", lambda e: e.scalar_tensor_tensor(
                out=dst[:, :], in0=t1[:, :], scalar=TWO_PI, in1=dst[:, :],
                op0=ALU.mult, op1=ALU.add), waits=[tn])
            tp = E("vector", lambda e: e.tensor_scalar(
                out=dst[:, :], in0=dst[:, :], scalar1=-math.pi, scalar2=math.pi,
                op0=ALU.max, op1=ALU.min), waits=[to])
            return E("scalar", lambda e: e.activation(
                out=dst[:, :], in_=dst[:, :], func=AF.Sin), waits=[tp])

        ts_ = sin_of(sn_tab, 0.0, tF)
        tc_ = sin_of(cs_tab, math.pi / 2.0, ts_)
        E("sync", lambda e: e.dma_start(
            out=self.rope_dram.ap()[0], in_=cs_tab[:, :]), dma=True,
            sem=self.s_b, waits=[tc_])
        E("sync", lambda e: e.dma_start(
            out=self.rope_dram.ap()[1], in_=sn_tab[:, :]), dma=True,
            sem=self.s_b, waits=[ts_])
        self.t_rope = self.pg.total(self.s_b)
        for g in (g5, g4, g3, g2, g1, g0b, g0a):
            g.__exit__(None, None, None)

    def ffn_weights(self):
        E, pg, nc = self.E, self.pg, self.nc
        g1, raw = self.sb_scope("qw_raw", [128, IF], F32)
        g2, q8 = self.sb_scope("qw_q8", [128, IF], FP8)
        g3, ws = self.sb_scope("qw_ws", [128, 8], F32)
        g4, tr8 = self.sb_scope("qw_tr", [128, 512], FP8)
        gp, ptr = self.ps_scope("qw_ptr", [128, 512])

        state = {"last_store": None, "last_q": None, "last_ws": None}

        def shard(src, n_pc, n_free, binT, ws_bin):
            last_store = state["last_store"]
            last_q = state["last_q"]
            last_ws = state["last_ws"]
            for pc in range(n_pc):
                tl = E("gpsimd", lambda e, pc=pc: e.dma_start(
                    out=raw[:, 0:n_free], in_=src[pc * 128:(pc + 1) * 128, :]),
                    dma=True, sem=self.s_wg4,
                    waits=[last_store, last_q, getattr(self, "t_rope", None)])
                tlt = pg.total(self.s_wg4)
                tm = E("vector", lambda e: e.tensor_reduce(
                    ws[:, 0:1], raw[:, 0:n_free], axis=AX, op=ALU.max,
                    apply_absolute_value=True), waits=[tlt])
                tm2 = E("vector", lambda e: e.tensor_scalar(
                    out=ws[:, 1:2], in0=ws[:, 0:1], scalar1=1e-30, scalar2=None,
                    op0=ALU.max), waits=[tm])
                tr = E("vector", lambda e: e.reciprocal(ws[:, 2:3], ws[:, 1:2]),
                       waits=[tm2])
                tq = E("vector", lambda e: e.tensor_scalar(
                    out=q8[:, 0:n_free], in0=raw[:, 0:n_free], scalar1=ws[:, 2:3],
                    scalar2=224.0, op0=ALU.mult, op1=ALU.mult), waits=[tr])
                last_q = tq
                twx = E("vector", lambda e: e.tensor_scalar(
                    out=ws[:, 3:4], in0=ws[:, 0:1], scalar1=1.0 / 448.0,
                    scalar2=None, op0=ALU.mult), waits=[tq, last_ws])
                last_ws = E("sync", lambda e, pc=pc: e.dma_start(
                    out=ws_bin.ap()[pc * 128:(pc + 1) * 128, :], in_=ws[:, 3:4]),
                    dma=True, sem=self.s_ws, waits=[twx])
                nfc = n_free // 128
                for fc0 in range(0, nfc, 4):
                    nb = min(4, nfc - fc0)
                    tts = None
                    for j in range(nb):
                        fc = fc0 + j
                        tts = E("tensor", lambda e, fc=fc, j=j: e.matmul(
                            ptr[:, j * 128:(j + 1) * 128],
                            q8[:, fc * 128:(fc + 1) * 128], self.ident_f8[:, :],
                            start=True, stop=True),
                            waits=[tq, self.t_id8, last_store])
                    tev = E("scalar", lambda e, nb=nb: e.copy(
                        out=tr8[:, 0:nb * 128], in_=ptr[:, 0:nb * 128]), waits=[tts])
                    last_store = E("sync", lambda e, pc=pc, fc0=fc0, nb=nb: e.dma_start(
                        out=binT.ap()[fc0 * 128:(fc0 + nb) * 128,
                                      pc * 128:(pc + 1) * 128].rearrange(
                                      "(n p) q -> p n q", p=128),
                        in_=tr8[:, 0:nb * 128].rearrange("p (n q) -> p n q", q=128)),
                        dma=True, sem=self.s_b, waits=[tev])

            state.update(last_store=last_store, last_q=last_q, last_ws=last_ws)

        shard(self.wg_s, ISH // 128, D, self.g_bin, self.wsg_bin)
        shard(self.wu_s, ISH // 128, D, self.u_bin, self.wsu_bin)
        shard(self.wd_s, HSH // 128, IF, self.d_bin, self.wsd_bin)
        t_all = pg.total(self.s_b)
        t_all_ws = pg.total(self.s_ws)
        for (i_, o_) in ((self.g_bin, self.g_bout), (self.u_bin, self.u_bout),
                         (self.d_bin, self.d_bout), (self.wsg_bin, self.wsg_bout),
                         (self.wsu_bin, self.wsu_bout), (self.wsd_bin, self.wsd_bout)):
            E("gpsimd", lambda e, i_=i_, o_=o_: e.collective_compute(
                "AllGather", ALU.bypass, replica_groups=self.ALL8,
                ins=[i_.ap().opt()], outs=[o_.ap().opt()]),
                waits=[t_all, t_all_ws], sem=self.s_cc)
        self.t_ffnw = pg.total(self.s_cc)
        for g in (gp, g4, g3, g2, g1):
            g.__exit__(None, None, None)

    def ln_stats(self, kind, tag):
        E = self.E
        ticks = []
        for c in range(TC):
            xc = self.x_res[:, c * D:(c + 1) * D]
            mu = self.stats[:, c:c + 1]
            rho = self.stats[:, 8 + c:9 + c]
            s1 = self.svec[:, 24:25]
            s2 = self.svec[:, 25:26]
            w = self.svec[:, 26:27]
            y = self.svec[:, 27:28]
            r2 = self.svec[:, 28:29]
            rv = self.svec[:, 29:30]
            prev = ticks[-1] if ticks else self.t_sv0
            ta = E("vector", lambda e, xc=xc, s1=s1: e.reduce_sum(
                s1, xc, axis=AX), waits=[self.t_xload, prev])
            tb = None
            for q4 in range(4):
                tb = E("scalar", lambda e, xc=xc, q4=q4: e.activation(
                    out=self.scr512[:, :], in_=xc[:, q4 * 512:(q4 + 1) * 512],
                    func=AF.Square, accum_out=self.svec[:, q4:q4 + 1]),
                    waits=[self.t_xload, prev, tb])
            ts_ = E("vector", lambda e, s2=s2: e.reduce_sum(
                s2, self.svec[:, 0:4], axis=AX), waits=[tb])
            tm2 = E("vector", lambda e, mu=mu, s1=s1: e.tensor_scalar(
                out=mu, in0=s1, scalar1=1.0 / D, scalar2=None, op0=ALU.mult),
                waits=[ta])
            td = E("vector", lambda e, w=w, mu=mu: e.tensor_tensor(
                out=w, in0=mu, in1=mu, op=ALU.mult), waits=[tm2])
            te = E("vector", lambda e, w=w, s2=s2: e.scalar_tensor_tensor(
                out=w, in0=s2, scalar=1.0 / D, in1=w, op0=ALU.mult,
                op1=ALU.subtract), waits=[td, ts_])
            tw3 = E("vector", lambda e, w=w: e.tensor_scalar(
                out=w, in0=w, scalar1=EPS, scalar2=None, op0=ALU.add), waits=[te])
            tg = E("scalar", lambda e, y=y, w=w: e.activation(
                out=y, in_=w, func=AF.Ln), waits=[tw3, tm2])
            th = E("scalar", lambda e, y=y: e.activation(
                out=y, in_=y, func=AF.Exp, scale=-0.5), waits=[tg])
            ti = E("vector", lambda e, r2=r2, y=y: e.tensor_tensor(
                out=r2, in0=y, in1=y, op=ALU.mult), waits=[th])
            tj = E("vector", lambda e, r2=r2, w=w: e.tensor_tensor(
                out=r2, in0=r2, in1=w, op=ALU.mult), waits=[ti])
            tk = E("vector", lambda e, r2=r2: e.tensor_scalar(
                out=r2, in0=r2, scalar1=-0.5, scalar2=1.5, op0=ALU.mult,
                op1=ALU.add), waits=[tj])
            tl = E("vector", lambda e, y=y, r2=r2: e.tensor_tensor(
                out=y, in0=y, in1=r2, op=ALU.mult), waits=[tk])
            if kind == "a":
                yp = self.stats[:, 16 + c:17 + c]
                tl = E("vector", lambda e, yp=yp, y=y: e.tensor_copy(
                    out=yp, in_=y), waits=[tl])
                tm_ = E("vector", lambda e, rv=rv, w=w: e.reciprocal(rv, w),
                        waits=[tw3, tl])
                tn = E("vector", lambda e, rv=rv: e.tensor_scalar(
                    out=rv, in0=rv, scalar1=-EPS, scalar2=1.0 + EPS,
                    op0=ALU.mult, op1=ALU.add), waits=[tm_])
                to = E("scalar", lambda e, rv=rv: e.activation(
                    out=rv, in_=rv, func=AF.Ln), waits=[tn])
                tp = E("scalar", lambda e, rv=rv: e.activation(
                    out=rv, in_=rv, func=AF.Exp, scale=-0.5), waits=[to])
                tq = E("vector", lambda e, rho=rho, y=y, rv=rv: e.tensor_tensor(
                    out=rho, in0=y, in1=rv, op=ALU.mult), waits=[tp, tl])
            else:
                tq = E("vector", lambda e, rho=rho, y=y: e.tensor_copy(
                    out=rho, in_=y), waits=[tl])
            ticks.append(tq)
        return ticks

    def attn(self, kind):
        E, pg, nc = self.E, self.pg, self.nc
        la = kind == "a"
        wsrc = self.win_s if la else self.wqkvT_s
        wo_src = self.wout_a if la else self.woutT_b

        t_ln = self.ln_stats("a" if la else "p", f"at{kind}")

        if la:
            gw, lnwB = self.sb_scope(f"lnwB{kind}", [128, D], F32)
            gb, lnbB = self.sb_scope(f"lnbB{kind}", [128, D], F32)
            E("sync", lambda e: e.dma_start(
                out=lnwB[:, :], in_=self.lnw_in[0:1, :].partition_broadcast(128)),
                dma=True, sem=self.s_w)
            E("sync", lambda e: e.dma_start(
                out=lnbB[:, :], in_=self.lnb_in[0:1, :].partition_broadcast(128)),
                dma=True, sem=self.s_w)
            t_lnt = pg.total(self.s_w)
        else:
            lnwB = lnbB = None
            t_lnt = None

        t_lnbv_bc = t_lnbq = None
        lnbVb = lnbQK = None
        gvb = gqk = None
        if la:
            gvb, lnbVb = self.sb_scope(f"a{kind}_lnbVb", [128, HM * DH], F32)
            gqk, lnbQK = self.sb_scope(f"a{kind}_lnbQK", [128, 16], F32)
            glb, lnb_dm = self.sb_scope("lnb_dm", [128, DC], BF16)
            glr, lnb_row = self.sb_scope("lnb_row", [1, 3 * HM * DH], F32)
            gwt, w_t = self.sb_scope("lnb_wt", [128, DC * 128], BF16)
            gpq, p_ln = self.ps_scope("p_lnb", [1, 512])
            tlb = E("gpsimd", lambda e: e.dma_start(
                out=lnb_dm[:, :],
                in_=self.lnb_in.rearrange("a (dc p) -> a p dc", p=128)[0],
                allow_slow_non_contiguous=True),
                dma=True, sem=self.s_wg)
            tlbt = pg.total(self.s_wg)
            ev = None
            for f in range(24):
                tl2 = E("gpsimd", lambda e, f=f: e.dma_start(
                    out=w_t[:, :].rearrange("p (dc q) -> p dc q", dc=DC), in_=wsrc[:, f * 128:(f + 1) * 128].rearrange(
                        "(dc p) q -> p dc q", p=128)), dma=True, sem=self.s_wg2,
                    waits=[ev])
                tl2t = pg.total(self.s_wg2)
                mm = None
                for dc in range(DC):
                    mm = E("tensor", lambda e, f=f, dc=dc: e.matmul(
                        p_ln[0:1, 0:128], lnb_dm[:, dc:dc + 1],
                        w_t[:, dc * 128:(dc + 1) * 128],
                        start=(dc == 0), stop=(dc == DC - 1)),
                        waits=[tlbt, tl2t, ev])
                ev = E("vector", lambda e, f=f: e.tensor_copy(
                    out=lnb_row[0:1, f * 128:(f + 1) * 128], in_=p_ln[0:1, 0:128]),
                    waits=[mm])
            tst = E("sync", lambda e: e.dma_start(
                out=self.vb.ap()[0:1, 0:3 * HM * DH], in_=lnb_row[0:1, :]),
                dma=True, sem=self.s_vb, waits=[ev])
            _t1 = E("sync", lambda e: e.dma_start(
                out=lnbQK[:, :],
                in_=self.vb.ap()[0:1, 0:2048].rearrange("a (f p) -> a p f", p=128)[0],
                allow_slow_non_contiguous=True),
                dma=True, sem=self.s_vb, waits=[tst])
            _t2 = E("sync", lambda e: e.dma_start(
                out=lnbVb[:, :],
                in_=self.vb.ap()[0:1, 2048:2048 + HM * DH].partition_broadcast(128)),
                dma=True, sem=self.s_vb, waits=[tst])
            t_lnbq = t_lnbv_bc = pg.total(self.s_vb)
            gpq.__exit__(None, None, None)
            gwt.__exit__(None, None, None)
            glr.__exit__(None, None, None)
            glb.__exit__(None, None, None)

        gxn, xn_tm = self.sb_scope(f"a{kind}_xn", [128, TC * D], BF16)
        t_xn = []
        for c in range(TC):
            xc = self.x_res[:, c * D:(c + 1) * D]
            mu = self.stats[:, c:c + 1]
            rho = self.stats[:, 8 + c:9 + c]
            dst = xn_tm[:, c * D:(c + 1) * D]
            if la:
                tv = E("vector", lambda e, xc=xc, mu=mu, rho=rho: e.tensor_scalar(
                    out=self.scrD[:, :], in0=xc, scalar1=mu, scalar2=rho,
                    op0=ALU.subtract, op1=ALU.mult),
                    waits=[t_ln[c], t_ln[-1], t_xn[-1] if t_xn else None])
                tw = E("vector", lambda e, dst=dst: e.tensor_tensor(
                    out=dst, in0=self.scrD[:, :], in1=lnwB[:, :], op=ALU.mult),
                    waits=[tv, t_lnt, t_lnbq])
                t_xn.append(tw)
            else:
                tv = E("vector", lambda e, xc=xc, mu=mu, rho=rho, dst=dst:
                       e.tensor_scalar(out=dst, in0=xc, scalar1=mu, scalar2=rho,
                                       op0=ALU.subtract, op1=ALU.mult),
                       waits=[t_ln[c], t_ln[-1]])
                t_xn.append(tv)

        gxm, xnT_m = self.sb_scope(f"a{kind}_xnTm", [128, DC * T], BF16)
        gpt, ptr = self.ps_scope(f"a{kind}_ptr", [128, 512])
        ev_prev = None
        for dc in range(DC):
            for t4 in range(0, TC, 4):
                tt = None
                for j in range(4):
                    c = t4 + j
                    tt = E("tensor", lambda e, c=c, dc=dc, j=j: e.matmul(
                        ptr[:, j * 128:(j + 1) * 128],
                        xn_tm[:, c * D + dc * 128:c * D + (dc + 1) * 128],
                        self.ident_bf[:, :], start=True, stop=True),
                        waits=[t_xn[c], self.t_id, ev_prev])
                ev_prev = E("scalar", lambda e, dc=dc, t4=t4: e.copy(
                    out=xnT_m[:, dc * T + t4 * 128:dc * T + (t4 + 4) * 128],
                    in_=ptr[:, :]), waits=[tt])
        t_xn_bout = E("sync", lambda e: e.dma_start(
            out=self.xn_bin.ap().rearrange("d p t -> p d t"),
            in_=xnT_m[:, :].rearrange("p (d t) -> p d t", d=DC)),
            dma=True, sem=self.s_b, waits=[ev_prev])
        t_ag_xn = E("gpsimd", lambda e: e.collective_compute(
            "AllGather", ALU.bypass, replica_groups=self.PAIRS,
            ins=[self.xn_bin.ap().opt()], outs=[self.xn_bout.ap().opt()]),
            waits=[pg.total(self.s_b)], sem=self.s_cc)
        t_spill = E("scalar", lambda e: e.dma_start(
            out=self.x_spill, in_=self.x_res[:, :]), dma=True, sem=self.s_sp,
            waits=[t_ln[-1], t_xn[-1]])
        gpt.__exit__(None, None, None)
        gxm.__exit__(None, None, None)
        gxn.__exit__(None, None, None)

        qg = self.x_res[:, 0:TC * D // 2].bitcast(BF16)
        kg = self.x_res[:, TC * D // 2:TC * D].bitcast(BF16)

        gv, v_tm = self.sb_scope(f"a{kind}_v", [128, KC * HM * DH], BF16)
        gxT, xnT = self.sb_scope(f"a{kind}_xnT", [128, DC * S], BF16)
        for sl in range(2):
            E("sync", lambda e, sl=sl: e.dma_start(
                out=xnT[:, :].rearrange("p (d h t) -> p d h t", d=DC, h=2)[:, :, sl, :],
                in_=self.xn_bout.ap()[sl].rearrange("d p t -> p d t")),
                dma=True, sem=self.s_gi, waits=[t_ag_xn])
        t_xnT = pg.total(self.s_gi)

        gwv, wv = self.sb_scope(f"a{kind}_wv", [128, DC * HM * DH // 2], BF16)
        gv0, pv0 = self.ps_scope(f"a{kind}_pv0", [128, 512])
        gv1, pv1 = self.ps_scope(f"a{kind}_pv1", [128, 512])
        ev_pv = [None, None]
        t_wv_t = None
        for fh in range(2):
            t_wv = E("gpsimd", lambda e, fh=fh: e.dma_start(
                out=wv[:, :].rearrange("p (dc f) -> p dc f", dc=DC),
                in_=wsrc[:, 2 * HM * DH + fh * 512:2 * HM * DH + (fh + 1) * 512]
                .rearrange("(dc p) f -> p dc f", p=128)),
                dma=True, sem=self.s_wg3,
                waits=[ev_pv[0], ev_pv[1]])
            t_wv_t = pg.total(self.s_wg3)
            for kc in range(KC):
                pv = pv0 if kc % 2 == 0 else pv1
                mm = None
                for dc in range(DC):
                    mm = E("tensor", lambda e, kc=kc, dc=dc, pv=pv, fh=fh: e.matmul(
                        pv[:, :],
                        xnT[:, dc * S + kc * 128:dc * S + (kc + 1) * 128],
                        wv[:, dc * 512:(dc + 1) * 512],
                        start=(dc == 0), stop=(dc == DC - 1)),
                        waits=[t_xnT, t_wv_t, ev_pv[kc % 2]])
                dst = v_tm[:, kc * HM * DH + fh * 512:kc * HM * DH + (fh + 1) * 512]
                if la:
                    ev_pv[kc % 2] = E("vector", lambda e, pv=pv, dst=dst, fh=fh:
                                      e.tensor_tensor(
                        out=dst, in0=pv[:, :],
                        in1=lnbVb[:, fh * 512:(fh + 1) * 512], op=ALU.add),
                        waits=[mm, t_lnbv_bc])
                else:
                    ev_pv[kc % 2] = E("vector", lambda e, pv=pv, dst=dst:
                                      e.tensor_copy(out=dst, in_=pv[:, :]),
                                      waits=[mm])
                t_v = pg.total("vector")
        gv1.__exit__(None, None, None)
        gv0.__exit__(None, None, None)
        gwv.__exit__(None, None, None)

        gw0, wqk0 = self.sb_scope(f"a{kind}_wqk0", [128, DC * 128], BF16)
        gw1, wqk1 = self.sb_scope(f"a{kind}_wqk1", [128, DC * 128], BF16)
        gq0, pq0 = self.ps_scope(f"a{kind}_pq0", [128, 2048])
        gq1, pq1 = self.ps_scope(f"a{kind}_pq1", [128, 2048])
        wbufs = [wqk0, wqk1]
        pbufs = [pq0, pq1]
        ev_qk = [None, None]
        t_qk = []
        for ft in range(16):
            par = ft % 2
            wb = wbufs[par]
            E("gpsimd", lambda e, ft=ft, wb=wb: e.dma_start(
                out=wb[:, :].rearrange("p (dc f) -> p dc f", dc=DC),
                in_=wsrc[:, ft * 128:(ft + 1) * 128].rearrange(
                    "(dc p) f -> p dc f", p=128)),
                dma=True, sem=self.s_wp[par],
                waits=[ev_qk[par], t_v if ft < 2 else None])
            twt = pg.total(self.s_wp[par])
            pq = pbufs[par]
            mm = None
            for sl2 in range(4):
                for dc in range(DC):
                    mm = E("tensor", lambda e, sl2=sl2, dc=dc, pq=pq, wb=wb: e.matmul(
                        pq[:, sl2 * 512:(sl2 + 1) * 512],
                        wb[:, dc * 128:(dc + 1) * 128],
                        xnT[:, dc * S + sl2 * 512:dc * S + (sl2 + 1) * 512],
                        start=(dc == 0), stop=(dc == DC - 1)),
                        waits=[twt, t_xnT, ev_qk[par], t_v if ft < 2 else None])
            dst_t = qg if ft < 8 else kg
            dst = dst_t[:, (ft % 8) * S:(ft % 8 + 1) * S]
            if la:
                ev_qk[par] = E("vector", lambda e, pq=pq, dst=dst, ft=ft:
                               e.tensor_scalar(
                    out=dst, in0=pq[:, :], scalar1=lnbQK[:, ft:ft + 1],
                    scalar2=None, op0=ALU.add), waits=[mm, t_lnbq, t_spill])
            else:
                ev_qk[par] = E("scalar", lambda e, pq=pq, dst=dst: e.copy(
                    out=dst, in_=pq[:, :]), waits=[mm, t_spill])
            t_qk.append(ev_qk[par])
        gq1.__exit__(None, None, None)
        gq0.__exit__(None, None, None)
        gw1.__exit__(None, None, None)
        gw0.__exit__(None, None, None)
        gxT.__exit__(None, None, None)

        if os.environ.get("MK_DEBUG") and la:
            E("gpsimd", lambda e: e.dma_start(
                out=self.dbg.ap()[5], in_=qg[:, 4 * S:5 * S]), dma=True,
                sem=self.s_wg, waits=[t_qk[4], t_qk[-1]])
            E("gpsimd", lambda e: e.dma_start(
                out=self.dbg.ap()[6], in_=xnT[:, 0:S]), dma=True,
                sem=self.s_wg, waits=[t_xnT, t_qk[-1]])
        t_rope_done = list(t_qk)
        rope_guards = []
        if la:
            gm1, m1 = self.sb_scope("rp_m1", [128, S], F32)
            gm2, m2 = self.sb_scope("rp_m2", [128, S], F32)
            gct, cs_t = self.sb_scope("rp_cst", [128, S], F32)
            gst, sn_t = self.sb_scope("rp_snt", [128, S], F32)
            rope_guards = [gst, gct, gm2, gm1]
            E("sync", lambda e: e.dma_start(
                out=cs_t[:, :], in_=self.rope_dram.ap()[0]), dma=True,
                sem=self.s_gi, waits=[self.t_rope, t_xnT, t_qk[-1],
                                      pg.total(self.s_wg)])
            E("sync", lambda e: e.dma_start(
                out=sn_t[:, :], in_=self.rope_dram.ap()[1]), dma=True,
                sem=self.s_gi, waits=[self.t_rope, t_xnT, t_qk[-1],
                                      pg.total(self.s_wg)])
            t_ropeload = pg.total(self.s_gi)
            last = None
            for tgt, base in ((qg, t_qk[0:8]), (kg, t_qk[8:16])):
                for eo in range(2):
                    et = tgt[:, eo * S:(eo + 1) * S]
                    ot = tgt[:, (2 + eo) * S:(3 + eo) * S]
                    w0 = [base[eo], base[2 + eo], t_ropeload, last]
                    ta = E("vector", lambda e, et=et: e.tensor_tensor(
                        out=m1[:, :], in0=et, in1=cs_t[:, :], op=ALU.mult), waits=w0)
                    tb = E("vector", lambda e, ot=ot: e.tensor_tensor(
                        out=m2[:, :], in0=ot, in1=sn_t[:, :], op=ALU.mult),
                        waits=[ta])
                    tc_ = E("vector", lambda e: e.tensor_sub(
                        out=m2[:, :], in0=m1[:, :], in1=m2[:, :]), waits=[tb])
                    td = E("vector", lambda e, ot=ot: e.tensor_tensor(
                        out=m1[:, :], in0=ot, in1=cs_t[:, :], op=ALU.mult),
                        waits=[tc_])
                    te = E("vector", lambda e, et=et, ot=ot: e.tensor_tensor(
                        out=ot, in0=et, in1=sn_t[:, :], op=ALU.mult), waits=[td])
                    tf = E("vector", lambda e, ot=ot: e.tensor_tensor(
                        out=ot, in0=ot, in1=m1[:, :], op=ALU.add), waits=[te])
                    tg = E("vector", lambda e, et=et: e.tensor_copy(
                        out=et, in_=m2[:, :]), waits=[tf])
                    last = tg
            t_rope_done = [last] + t_qk

        gq2a, q2a = self.sb_scope(f"a{kind}_q2a", [128, S], BF16)
        gq2b, q2b = self.sb_scope(f"a{kind}_q2b", [128, S], BF16)
        gk2a, k2a = self.sb_scope(f"a{kind}_k2a", [128, S], BF16)
        gk2b, k2b = self.sb_scope(f"a{kind}_k2b", [128, S], BF16)
        gpt0, psA = self.ps_scope(f"a{kind}_psA", [128, 1024])
        gpt1, psB = self.ps_scope(f"a{kind}_psB", [128, 1024])
        gpc, psC = self.ps_scope(f"a{kind}_psC", [128, 1024])
        gpz, psZ = self.ps_scope(f"a{kind}_psZ", [1, 1024])
        gsq, sqb = self.sb_scope(f"a{kind}_sq", [128, S], BF16)
        gpp0, pt0 = self.sb_scope(f"a{kind}_pt0", [128, 1024], BF16)
        gpp1, pt1 = self.sb_scope(f"a{kind}_pt1", [128, 1024], BF16)
        gzr, zrow = self.sb_scope(f"a{kind}_zrow", [1, 1024], F32)
        gzb, zb = self.sb_scope(f"a{kind}_zb", [128, 1024], F32)
        gbb, biasb = self.sb_scope(f"a{kind}_bias", [128, 2], F32)
        gce0, ctxe0 = self.sb_scope(f"a{kind}_ce0", [128, 1024], BF16)
        gce1, ctxe1 = self.sb_scope(f"a{kind}_ce1", [128, 1024], BF16)
        q2s, k2s = [q2a, q2b], [k2a, k2b]
        ctxes = [ctxe0, ctxe1]
        t_rep_free = [None, None]
        t_bias_free = [None, None]
        t_ce_free = [None, None]
        t_ctx_ev = []
        t_zrow_free = None
        t_zb_free = None
        t_sq_free = None
        t_psz_free = None
        prev_exp = [None, None]
        prev_pt_use = [None, None]
        t_qk_all = t_qk[-1]
        ce_i = 0
        for h in range(HM):
            par = h % 2
            if la:
                q2, k2 = q2s[par][:, :], k2s[par][:, :]
                segs = [(0, 32, h // 4, 32 * (h % 4)),
                        (32, 64, 2 + h // 4, 32 * (h % 4)),
                        (64, 128, 4 + h // 2, 64 * (h % 2))]
                for (d0, d1, tile, p0) in segs:
                    for dst_t, src_t in ((q2, qg), (k2, kg)):
                        E("sync", lambda e, d0=d0, d1=d1, tile=tile, p0=p0,
                          dst_t=dst_t, src_t=src_t: e.dma_start(
                            out=dst_t[d0:d1, :],
                            in_=src_t[p0:p0 + (d1 - d0), tile * S:(tile + 1) * S]),
                            dma=True, sem=self.s_rp[par],
                            waits=[t_rope_done[0], t_rep_free[par]])
                t_rp = pg.total(self.s_rp[par])
                t_rp2 = t_rp
            else:
                q2 = qg[:, h * S:(h + 1) * S]
                k2 = kg[:, h * S:(h + 1) * S]
                t_rp = t_qk[h]
                t_rp2 = t_qk[8 + h]
            for side, src_ in ((0, q2), (1, k2)):
                tsq = E("vector", lambda e, src_=src_: e.tensor_tensor(
                    out=sqb[:, :], in0=src_, in1=src_, op=ALU.mult),
                    waits=[t_rp, t_rp2, t_sq_free,
                           t_qk_all if h == 0 else None])
                for half in range(2):
                    mm = None
                    for sl2 in range(2):
                        mm = E("tensor", lambda e, half=half, sl2=sl2: e.matmul(
                            psZ[0:1, sl2 * 512:(sl2 + 1) * 512],
                            self.ones_col_bf[:, :],
                            sqb[:, half * 1024 + sl2 * 512:
                                half * 1024 + (sl2 + 1) * 512],
                            start=True, stop=True),
                            waits=[tsq, self.t_ones, t_psz_free,
                                   t_qk_all if h == 0 else None])
                    t_sq_free = mm
                    col = self.tiny[0:1, 8 + side * 2 + half:9 + side * 2 + half]
                    t_psz_free = E("vector", lambda e, col=col: e.tensor_reduce(
                        col, psZ[0:1, :], axis=AX, op=ALU.max), waits=[mm])
            tmx = E("vector", lambda e: e.tensor_reduce(
                self.tiny[0:1, 12:13], self.tiny[0:1, 8:10], axis=AX, op=ALU.max),
                waits=[t_psz_free])
            tmx2 = E("vector", lambda e: e.tensor_reduce(
                self.tiny[0:1, 13:14], self.tiny[0:1, 10:12], axis=AX, op=ALU.max),
                waits=[tmx])
            tpd = E("vector", lambda e: e.tensor_tensor(
                out=self.tiny[0:1, 14:15], in0=self.tiny[0:1, 12:13],
                in1=self.tiny[0:1, 13:14], op=ALU.mult), waits=[tmx2])
            tsq2 = E("scalar", lambda e: e.activation(
                out=self.tiny[0:1, 15:16], in_=self.tiny[0:1, 14:15], func=AF.Sqrt,
                scale=(RSQ * 1.01) ** 2), waits=[tpd])
            tng = E("vector", lambda e: e.tensor_scalar(
                out=self.tiny[0:1, 16:17], in0=self.tiny[0:1, 15:16], scalar1=-1.0,
                scalar2=None, op0=ALU.mult), waits=[tsq2])
            tvb = E("sync", lambda e: e.dma_start(
                out=self.vb.ap()[2:3, 0:1], in_=self.tiny[0:1, 16:17]),
                dma=True, sem=self.s_vb, waits=[tng, pg.total(self.s_vb)])
            t_bias = E("sync", lambda e, par=par: e.dma_start(
                out=biasb[:, par:par + 1],
                in_=self.vb.ap()[2:3, 0:1].partition_broadcast(128)),
                dma=True, sem=self.s_vb, waits=[tvb, t_bias_free[par]])
            if os.environ.get("MK_DEBUG") and h == 0 and la:
                E("gpsimd", lambda e, q2=q2: e.dma_start(
                    out=self.dbg.ap()[0], in_=q2), dma=True, sem=self.s_wg,
                    waits=[t_rp, t_rp2])
                E("gpsimd", lambda e, k2=k2: e.dma_start(
                    out=self.dbg.ap()[1], in_=k2), dma=True, sem=self.s_wg,
                    waits=[t_rp, t_rp2])
            for qh in range(2):
                mm_z = None
                mm_c = None
                for kc in range(KC):
                    kpar = kc % 2
                    psS = psA if kpar == 0 else psB
                    ptb = pt0 if kpar == 0 else pt1
                    mm_s = None
                    for sl2 in range(2):
                        mm_s = E("tensor", lambda e, kc=kc, qh=qh, sl2=sl2, psS=psS,
                                 q2=q2, k2=k2: e.matmul(
                            psS[:, sl2 * 512:(sl2 + 1) * 512],
                            k2[:, kc * 128:(kc + 1) * 128],
                            q2[:, qh * 1024 + sl2 * 512:qh * 1024 + (sl2 + 1) * 512],
                            start=True, stop=True),
                            waits=[t_rp, t_rp2, prev_exp[kpar], t_bias])
                    t_exp = E("scalar", lambda e, psS=psS, ptb=ptb, par=par:
                              e.activation(
                        out=ptb[:, :], in_=psS[:, :], func=AF.Exp, scale=RSQ,
                        bias=biasb[:, par:par + 1]),
                        waits=[mm_s, t_bias, prev_pt_use[kpar]])
                    prev_exp[kpar] = t_exp
                    t_bias_free[par] = t_exp
                    if os.environ.get("MK_DEBUG") and h == 0 and qh == 0 \
                            and kc == 0 and la:
                        t_exp = E("gpsimd", lambda e, ptb=ptb: e.dma_start(
                            out=self.dbg.ap()[3][:, 0:1024], in_=ptb[:, :]),
                            dma=True, sem=self.s_wg, waits=[t_exp])
                        prev_exp[kpar] = t_exp
                    for sl2 in range(2):
                        mm_z = E("tensor", lambda e, kc=kc, sl2=sl2, ptb=ptb: e.matmul(
                            psZ[0:1, sl2 * 512:(sl2 + 1) * 512],
                            self.ones_col_bf[:, :],
                            ptb[:, sl2 * 512:(sl2 + 1) * 512],
                            start=(kc == 0), stop=(kc == KC - 1)),
                            waits=[t_exp, self.t_ones,
                                   t_psz_free if kc == 0 else None])
                    for sl2 in range(2):
                        mm_c = E("tensor", lambda e, kc=kc, sl2=sl2, h=h, ptb=ptb:
                                 e.matmul(
                            psC[:, sl2 * 512:(sl2 + 1) * 512],
                            v_tm[:, kc * HM * DH + h * DH:kc * HM * DH + (h + 1) * DH],
                            ptb[:, sl2 * 512:(sl2 + 1) * 512],
                            start=(kc == 0), stop=(kc == KC - 1)),
                            waits=[t_exp, t_v,
                                   t_ctx_ev[-1] if (kc == 0 and t_ctx_ev) else None])
                    prev_pt_use[kpar] = mm_c
                tze = E("vector", lambda e: e.tensor_copy(
                    out=zrow[0:1, :], in_=psZ[0:1, :]), waits=[mm_z, t_zrow_free])
                t_psz_free = tze
                if os.environ.get("MK_DEBUG") and h == 0 and qh == 0 and la:
                    tze = E("gpsimd", lambda e: e.dma_start(
                        out=self.dbg.ap()[4][0:1, 0:1024], in_=zrow[0:1, :]),
                        dma=True, sem=self.s_wg, waits=[tze])
                trc = E("vector", lambda e: e.reciprocal(zrow[0:1, :], zrow[0:1, :]),
                        waits=[tze])
                tzs = E("sync", lambda e: e.dma_start(
                    out=self.vb.ap()[3:4, 0:1024], in_=zrow[0:1, :]),
                    dma=True, sem=self.s_vb, waits=[trc, pg.total(self.s_vb)])
                t_zrow_free = tzs
                tzb = E("sync", lambda e: e.dma_start(
                    out=zb[:, :],
                    in_=self.vb.ap()[3:4, 0:1024].partition_broadcast(128)),
                    dma=True, sem=self.s_vb, waits=[tzs, t_zb_free])
                cpar = ce_i % 2
                ce = ctxes[cpar]
                tce = E("vector", lambda e, ce=ce: e.tensor_tensor(
                    out=ce[:, :], in0=psC[:, :], in1=zb[:, :], op=ALU.mult),
                    waits=[mm_c, tzb, t_ce_free[cpar]])
                t_zb_free = tce
                tcd = E("sync", lambda e, h=h, qh=qh, ce=ce: e.dma_start(
                    out=self.ctx_bin.ap()[h][:, qh * 1024:(qh + 1) * 1024],
                    in_=ce[:, :]), dma=True, sem=self.s_cb[cpar], waits=[tce])
                t_ce_free[cpar] = tcd
                ce_i += 1
                t_ctx_ev.append(tce)
            t_rep_free[par] = t_ctx_ev[-1]
        t_ctx_out = pg.total(self.s_cb[0])
        t_ctx_out2 = pg.total(self.s_cb[1])
        t_attn_done = t_ctx_ev[-1]
        for g in (gce1, gce0, gbb, gzb, gzr, gpp1, gpp0, gsq, gpz, gpc,
                  gpt1, gpt0, gk2b, gk2a, gq2b, gq2a):
            g.__exit__(None, None, None)
        for g in rope_guards:
            g.__exit__(None, None, None)
        gv.__exit__(None, None, None)
        if la:
            gqk.__exit__(None, None, None)
            gvb.__exit__(None, None, None)

        t_ag_c = E("gpsimd", lambda e: e.collective_compute(
            "AllGather", ALU.bypass, replica_groups=self.PAIRS,
            ins=[self.ctx_bin.ap().opt()], outs=[self.ctx_bout.ap().opt()]),
            waits=[t_ctx_out, t_ctx_out2], sem=self.s_cc)

        gca, ctx_all = self.sb_scope(f"a{kind}_ctxall", [128, NH * T], BF16)
        for gh in range(NH):
            E("gpsimd", lambda e, gh=gh: e.indirect_dma_start(
                out=ctx_all[:, gh * T:(gh + 1) * T], out_offset=None,
                in_=self.ctx_bout.ap(),
                in_offset=IndirectOffsetOnAxis(ap=self.cidx[:, gh:gh + 1], axis=0)),
                dma=True, sem=self.s_gig, waits=[t_ag_c, pg.total(self.s_w)])
        t_ctxall = pg.total(self.s_gig)

        t_xre = E("scalar", lambda e: e.dma_start(
            out=self.x_res[:, :], in_=self.x_spill), dma=True, sem=self.s_sp,
            waits=[pg.total(self.s_sp), t_attn_done, t_rope_done[0],
                   pg.total(self.s_rp[0]), pg.total(self.s_rp[1])])
        t_xrel = pg.total(self.s_sp)
        gwo0, wo0 = self.sb_scope(f"a{kind}_wo0", [128, D], BF16)
        gwo1, wo1 = self.sb_scope(f"a{kind}_wo1", [128, D], BF16)
        gpo0, po0 = self.ps_scope(f"a{kind}_po0", [128, 2048])
        gpo1, po1 = self.ps_scope(f"a{kind}_po1", [128, 2048])
        wos, pos = [wo0, wo1], [po0, po1]
        ev_o = [None, None]
        t_wuse = [None, None]
        for rnd in range(TC // 2):
            last_mm = [None, None]
            for dc in range(DC):
                wpar = dc % 2
                wob = wos[wpar]
                E("gpsimd", lambda e, dc=dc, wob=wob: e.dma_start(
                    out=wob[:, :], in_=wo_src[dc * 128:(dc + 1) * 128, :]),
                    dma=True, sem=self.s_wp[wpar], waits=[t_wuse[wpar]])
                twt = pg.total(self.s_wp[wpar])
                for tpar in range(2):
                    tc2 = rnd * 2 + tpar
                    po = pos[tpar]
                    for sl2 in range(4):
                        last_mm[tpar] = E("tensor",
                            lambda e, dc=dc, tc2=tc2, sl2=sl2, po=po, wob=wob:
                            e.matmul(
                                po[:, sl2 * 512:(sl2 + 1) * 512],
                                ctx_all[:, dc * T + tc2 * 128:dc * T + (tc2 + 1) * 128],
                                wob[:, sl2 * 512:(sl2 + 1) * 512],
                                start=(dc == 0), stop=(dc == DC - 1)),
                            waits=[twt, t_ctxall, ev_o[tpar],
                                   t_attn_done if (rnd == 0 and dc == 0) else None])
                t_wuse[wpar] = last_mm[1]
            for tpar in range(2):
                tc2 = rnd * 2 + tpar
                po = pos[tpar]
                xc = self.x_res[:, tc2 * D:(tc2 + 1) * D]
                mu = self.stats[:, tc2:tc2 + 1]
                rho = self.stats[:, 8 + tc2:9 + tc2]
                if la:
                    yp = self.stats[:, 16 + tc2:17 + tc2]
                    tv1 = E("vector", lambda e, xc=xc, mu=mu, yp=yp: e.tensor_scalar(
                        out=self.scrD[:, :], in0=xc, scalar1=mu, scalar2=yp,
                        op0=ALU.subtract, op1=ALU.mult),
                        waits=[last_mm[tpar], t_xrel, ev_o[0], ev_o[1]])
                    tv4 = E("vector", lambda e, po=po: e.tensor_tensor(
                        out=self.scrD[:, :], in0=self.scrD[:, :], in1=po[:, :],
                        op=ALU.add), waits=[tv1])
                    ev_o[tpar] = E("vector", lambda e, xc=xc: e.tensor_tensor(
                        out=xc, in0=xc, in1=self.scrD[:, :], op=ALU.add), waits=[tv4])
                else:
                    ev_o[tpar] = E("vector", lambda e, xc=xc, po=po: e.tensor_tensor(
                        out=xc, in0=xc, in1=po[:, :], op=ALU.add),
                        waits=[last_mm[tpar], t_xrel])
        for g in (gpo1, gpo0, gwo1, gwo0, gca):
            g.__exit__(None, None, None)
        if la:
            gb.__exit__(None, None, None)
            gw.__exit__(None, None, None)
        self.pg.barrier()

    def ffn(self, idx):
        E, pg, nc = self.E, self.pg, self.nc
        t_ln = self.ln_stats("p", f"f{idx}")

        gwsg, wsg = self.sb_scope(f"f{idx}_wsg", [128, IPAD // 128], F32)
        gwsu, wsu = self.sb_scope(f"f{idx}_wsu", [128, IPAD // 128], F32)
        gwsh, wshB = self.sb_scope(f"f{idx}_wshB", [128, D], F32)
        E("sync", lambda e: e.dma_start(
            out=wsg[:, :], in_=self.wsg_bout.ap().rearrange(
                "(c p) a -> a p c", p=128)[0],
            allow_slow_non_contiguous=True), dma=True, sem=self.s_w,
            waits=[self.t_ffnw])
        E("sync", lambda e: e.dma_start(
            out=wsu[:, :], in_=self.wsu_bout.ap().rearrange(
                "(c p) a -> a p c", p=128)[0],
            allow_slow_non_contiguous=True), dma=True, sem=self.s_w,
            waits=[self.t_ffnw])
        E("sync", lambda e: e.dma_start(
            out=wshB[:, :], in_=self.wsd_bout.ap().rearrange(
                "d c -> c d").partition_broadcast(128)),
            dma=True, sem=self.s_w, waits=[self.t_ffnw])
        t_ws = pg.total(self.s_w)

        gqxT, qxT = self.sb_scope(f"f{idx}_qxT", [128, DC * G], FP8)
        gtb, t_i = self.sb_scope(f"f{idx}_t", [128, IC * G], BF16)
        gqt, qt_i = self.sb_scope(f"f{idx}_qt", [128, IC * G], FP8)
        gqx, qx_tm = self.sb_scope(f"f{idx}_qx", [128, GC * D], FP8)
        gsA, Atile = self.sb_scope(f"f{idx}_At", [128, G], F32)
        gsBp, Bptile = self.sb_scope(f"f{idx}_Bp", [128, G], F32)
        gmac, macc = self.sb_scope(f"f{idx}_macc", [128, G], F32)
        gmac2, macc2 = self.sb_scope(f"f{idx}_macc2", [128, G], F32)
        ggs0, gsl0 = self.sb_scope(f"f{idx}_gs0", [128, DC * 128], FP8)
        ggs1, gsl1 = self.sb_scope(f"f{idx}_gs1", [128, DC * 128], FP8)
        gus0, usl0 = self.sb_scope(f"f{idx}_us0", [128, DC * 128], FP8)
        gus1, usl1 = self.sb_scope(f"f{idx}_us1", [128, DC * 128], FP8)
        gds0, dsl0 = self.sb_scope(f"f{idx}_ds0", [128, D], FP8)
        gds1, dsl1 = self.sb_scope(f"f{idx}_ds1", [128, D], FP8)
        gscr, scrG = self.sb_scope(f"f{idx}_scrG", [128, 3 * 512], F32)
        gpg0, psg0 = self.ps_scope(f"f{idx}_psg0", [128, 512])
        gpg1, psg1 = self.ps_scope(f"f{idx}_psg1", [128, 512])
        gpu0, psu0 = self.ps_scope(f"f{idx}_psu0", [128, 512])
        gpu1, psu1 = self.ps_scope(f"f{idx}_psu1", [128, 512])
        gpd, psD = self.ps_scope(f"f{idx}_psD", [128, 2048])
        ptr = psD

        t_group_done = None
        t_down_mm_done = None
        for g in range(FG):
            t_qx = []
            t_acol = []
            for j in range(GC):
                c = g * GC + j
                xc = self.x_res[:, c * D:(c + 1) * D]
                mu = self.stats[:, c:c + 1]
                rho = self.stats[:, 8 + c:9 + c]
                mx = self.stats[:, 24:25]
                sc_ = self.stats[:, 25:26]
                acol = self.stats[:, 16 + j:17 + j]
                tv1 = E("vector", lambda e, xc=xc, mu=mu: e.tensor_scalar(
                    out=self.scrD[:, :], in0=xc, scalar1=mu, scalar2=None,
                    op0=ALU.subtract), waits=[t_ln[c], t_group_done,
                                              t_qx[-1] if t_qx else None])
                tv2 = E("vector", lambda e, mx=mx: e.tensor_reduce(
                    mx, self.scrD[:, :], axis=AX, op=ALU.max,
                    apply_absolute_value=True), waits=[tv1])
                tv2b = E("vector", lambda e, mx=mx: e.tensor_scalar(
                    out=mx, in0=mx, scalar1=1e-30, scalar2=None, op0=ALU.max),
                    waits=[tv2])
                tv3 = E("vector", lambda e, sc_=sc_, mx=mx: e.reciprocal(sc_, mx),
                        waits=[tv2b])
                tv4 = E("vector", lambda e, j=j, sc_=sc_: e.tensor_scalar(
                    out=qx_tm[:, j * D:(j + 1) * D], in0=self.scrD[:, :],
                    scalar1=sc_, scalar2=224.0, op0=ALU.mult, op1=ALU.mult),
                    waits=[tv3])
                ta = E("vector", lambda e, acol=acol, mx=mx, rho=rho: e.tensor_scalar(
                    out=acol, in0=mx, scalar1=rho, scalar2=4.0 / 448.0,
                    op0=ALU.mult, op1=ALU.mult), waits=[tv3, tv4])
                t_qx.append(tv4)
                t_acol.append(ta)
            for j in range(GC):
                E("sync", lambda e, j=j, g=g: e.dma_start(
                    out=self.vb.ap()[5 + g:6 + g, j * 128:(j + 1) * 128],
                    in_=self.stats[:, 16 + j:17 + j]), dma=True, sem=self.s_vb,
                    waits=[t_acol[j], pg.total(self.s_vb)])
            t_At = E("sync", lambda e, g=g: e.dma_start(
                out=Atile[:, :],
                in_=self.vb.ap()[5 + g:6 + g, 0:G].partition_broadcast(128)),
                dma=True, sem=self.s_vb, waits=[pg.total(self.s_vb)])
            ev_tr = None
            for dc in range(DC):
                tt = None
                for j in range(GC):
                    tt = E("tensor", lambda e, j=j, dc=dc: e.matmul(
                        ptr[:, j * 128:(j + 1) * 128],
                        qx_tm[:, j * D + dc * 128:j * D + (dc + 1) * 128],
                        self.ident_f8[:, :], start=True, stop=True),
                        waits=[t_qx[j], self.t_id8, ev_tr, t_group_done])
                ev_tr = E("scalar", lambda e, dc=dc: e.copy(
                    out=qxT[:, dc * G:(dc + 1) * G], in_=ptr[:, 0:G]), waits=[tt])
            t_qxT = ev_tr

            gslabs, uslabs = [gsl0, gsl1], [usl0, usl1]
            psgs, psus = [psg0, psg1], [psu0, psu1]
            ev_g = [None, None]
            t_slab_free = [None, None]
            t_tmax = None
            t_ti = []
            for ic in range(IC):
                par = ic % 2
                gslab, uslab = gslabs[par], uslabs[par]
                rk, lc = ic // 6, ic % 6
                E("sync", lambda e, rk=rk, lc=lc, gslab=gslab: e.dma_start(
                    out=gslab[:, :].rearrange("p (hc i) -> p hc i", hc=DC),
                    in_=self.g_bout.ap()[rk, :, lc * 128:(lc + 1) * 128].rearrange(
                        "(hc p) i -> p hc i", p=128)),
                    dma=True, sem=self.s_f[par],
                    waits=[t_slab_free[par], self.t_ffnw])
                E("sync", lambda e, rk=rk, lc=lc, uslab=uslab: e.dma_start(
                    out=uslab[:, :].rearrange("p (hc i) -> p hc i", hc=DC),
                    in_=self.u_bout.ap()[rk, :, lc * 128:(lc + 1) * 128].rearrange(
                        "(hc p) i -> p hc i", p=128)),
                    dma=True, sem=self.s_f[par],
                    waits=[t_slab_free[par], self.t_ffnw])
                t_sl = pg.total(self.s_f[par])
                psg, psu = psgs[par], psus[par]
                mm_g = mm_u = None
                for hc in range(DC):
                    mm_g = E("tensor", lambda e, hc=hc, psg=psg, gslab=gslab: e.matmul(
                        psg[:, :], gslab[:, hc * 128:(hc + 1) * 128],
                        qxT[:, hc * G:(hc + 1) * G],
                        start=(hc == 0), stop=(hc == DC - 1)),
                        waits=[t_sl, t_qxT, ev_g[par]])
                for hc in range(DC):
                    mm_u = E("tensor", lambda e, hc=hc, psu=psu, uslab=uslab: e.matmul(
                        psu[:, :], uslab[:, hc * 128:(hc + 1) * 128],
                        qxT[:, hc * G:(hc + 1) * G],
                        start=(hc == 0), stop=(hc == DC - 1)),
                        waits=[t_sl, t_qxT, ev_g[par]])
                t_slab_free[par] = mm_u
                s_g1 = scrG[:, 0:512]
                s_u1 = scrG[:, 512:1024]
                s_sl = scrG[:, 1024:1536]
                te1 = E("scalar", lambda e, ic=ic, psg=psg, s_g1=s_g1: e.activation(
                    out=s_g1, in_=psg[:, :], func=AF.Copy,
                    scale=wsg[:, ic:ic + 1]),
                    waits=[mm_g, t_ws, t_ti[-1] if t_ti else None])
                te2 = E("vector", lambda e, s_g1=s_g1: e.tensor_tensor(
                    out=s_g1, in0=s_g1, in1=Atile[:, :], op=ALU.mult),
                    waits=[te1, t_At])
                te3 = E("scalar", lambda e, s_g1=s_g1, s_sl=s_sl: e.activation(
                    out=s_sl, in_=s_g1, func=AF.Silu), waits=[te2])
                te4 = E("scalar", lambda e, ic=ic, psu=psu, s_u1=s_u1: e.activation(
                    out=s_u1, in_=psu[:, :], func=AF.Copy,
                    scale=wsu[:, ic:ic + 1]), waits=[mm_u, t_ws, te3])
                te5 = E("vector", lambda e, ic=ic, s_sl=s_sl, s_u1=s_u1:
                        e.tensor_tensor(
                    out=t_i[:, ic * G:(ic + 1) * G], in0=s_sl, in1=s_u1,
                    op=ALU.mult), waits=[te3, te4])
                ev_g[par] = te5
                t_ti.append(te5)
                if ic == 0:
                    ta1 = E("vector", lambda e, ic=ic: e.tensor_copy(
                        out=macc[:, :], in_=t_i[:, ic * G:(ic + 1) * G]),
                        waits=[te5, t_group_done])
                    t_tmax = E("vector", lambda e, ic=ic: e.tensor_copy(
                        out=macc2[:, :], in_=t_i[:, ic * G:(ic + 1) * G]),
                        waits=[te5, ta1])
                else:
                    ta1 = E("vector", lambda e, ic=ic: e.tensor_tensor(
                        out=macc[:, :], in0=macc[:, :],
                        in1=t_i[:, ic * G:(ic + 1) * G], op=ALU.max),
                        waits=[te5, t_tmax])
                    t_tmax = E("vector", lambda e, ic=ic: e.tensor_tensor(
                        out=macc2[:, :], in0=macc2[:, :],
                        in1=t_i[:, ic * G:(ic + 1) * G], op=ALU.min),
                        waits=[te5, ta1])
            tn1 = E("vector", lambda e: e.tensor_scalar(
                out=macc2[:, :], in0=macc2[:, :], scalar1=-1.0, scalar2=None,
                op0=ALU.mult), waits=[t_tmax])
            t_tmax = E("vector", lambda e: e.tensor_tensor(
                out=macc[:, :], in0=macc[:, :], in1=macc2[:, :], op=ALU.max),
                waits=[tn1])
            t_m = []
            tt_prev = None
            for j in range(GC):
                tt = E("tensor", lambda e, j=j: e.matmul(
                    ptr[:, 0:128], macc[:, j * 128:(j + 1) * 128],
                    self.ident_f32[:, :], start=True, stop=True),
                    waits=[t_tmax, self.t_id32,
                           t_m[-1] if t_m else t_qxT, t_down_mm_done])
                col = self.stats[:, 26 + j:27 + j]
                tm_ = E("vector", lambda e, col=col: e.tensor_reduce(
                    col, ptr[:, 0:128], axis=AX, op=ALU.max), waits=[tt])
                tm2 = E("vector", lambda e, col=col: e.tensor_scalar(
                    out=col, in0=col, scalar1=1e-30, scalar2=None, op0=ALU.max),
                    waits=[tm_])
                t_m.append(tm2)
            t_b = []
            last_rs = None
            for j in range(GC):
                col = self.stats[:, 26 + j:27 + j]
                acol = self.stats[:, 16 + j:17 + j]
                bcol = self.stats[:, 20 + j:21 + j]
                tb1 = E("vector", lambda e, bcol=bcol, col=col, acol=acol:
                        e.tensor_scalar(
                    out=bcol, in0=col, scalar1=acol, scalar2=4.0 / 448.0,
                    op0=ALU.mult, op1=ALU.mult), waits=[t_m[j]])
                t_b.append(tb1)
                rcol = self.stats[:, 30:31]
                tr1 = E("vector", lambda e, rcol=rcol, col=col: e.reciprocal(
                    rcol, col), waits=[tb1, last_rs])
                tr2 = E("vector", lambda e, rcol=rcol: e.tensor_scalar(
                    out=rcol, in0=rcol, scalar1=224.0, scalar2=None, op0=ALU.mult),
                    waits=[tr1])
                last_rs = E("sync", lambda e, j=j, g=g, rcol=rcol: e.dma_start(
                    out=self.vb.ap()[7 + g:8 + g, j * 128:(j + 1) * 128], in_=rcol),
                    dma=True, sem=self.s_vb, waits=[tr2, pg.total(self.s_vb)])
            t_Bp = E("sync", lambda e, g=g: e.dma_start(
                out=Bptile[:, :],
                in_=self.vb.ap()[7 + g:8 + g, 0:G].partition_broadcast(128)),
                dma=True, sem=self.s_vb, waits=[pg.total(self.s_vb)])
            t_qt = []
            for ic in range(IC):
                t_qt.append(E("vector", lambda e, ic=ic: e.tensor_tensor(
                    out=qt_i[:, ic * G:(ic + 1) * G], in0=t_i[:, ic * G:(ic + 1) * G],
                    in1=Bptile[:, :], op=ALU.mult), waits=[t_Bp, t_ti[ic]]))
            dslabs = [dsl0, dsl1]
            t_dfree = [None, None]
            ev_d = None
            for jp in range(GC // 2):
                mm_d = [[None] * 4, [None] * 4]
                for ic in range(IC):
                    par = ic % 2
                    dslab = dslabs[par]
                    E("sync", lambda e, ic=ic, dslab=dslab: e.dma_start(
                        out=dslab[:, :].rearrange("p (r q) -> p r q", r=NCORE),
                        in_=self.d_bout.ap()[:, ic * 128:(ic + 1) * 128, :]
                        .rearrange("r p q -> p r q")),
                        dma=True, sem=self.s_fd[par],
                        waits=[t_dfree[par], self.t_ffnw])
                    t_dl = pg.total(self.s_fd[par])
                    for jj in range(2):
                        j = jp * 2 + jj
                        for sl2 in range(4):
                            pd_sl = (psD[:, sl2 * 512:(sl2 + 1) * 512] if jj == 0
                                     else [psg0, psg1, psu0, psu1][sl2][:, :])
                            mm_d[jj][sl2] = E("tensor",
                                lambda e, ic=ic, j=j, sl2=sl2, dslab=dslab,
                                pd_sl=pd_sl: e.matmul(
                                    pd_sl,
                                    qt_i[:, ic * G + j * 128:ic * G + (j + 1) * 128],
                                    dslab[:, sl2 * 512:(sl2 + 1) * 512],
                                    start=(ic == 0), stop=(ic == IC - 1)),
                                waits=[t_dl, t_qt[ic], ev_d,
                                       (t_m[-1] if (jp == 0 and ic == 0) else None),
                                       (t_ti[-1] if (jp == 0 and ic == 0) else None)])
                    t_dfree[par] = mm_d[1][3]
                t_down_mm_done = mm_d[1][3]
                for jj in range(2):
                    j = jp * 2 + jj
                    c = g * GC + j
                    bcol = self.stats[:, 20 + j:21 + j]
                    for sl2 in range(4):
                        pd_sl = (psD[:, sl2 * 512:(sl2 + 1) * 512] if jj == 0
                                 else [psg0, psg1, psu0, psu1][sl2][:, :])
                        xs = self.x_res[:, c * D + sl2 * 512:c * D + (sl2 + 1) * 512]
                        td1 = E("vector", lambda e, bcol=bcol, pd_sl=pd_sl, sl2=sl2:
                                e.scalar_tensor_tensor(
                            out=self.scrD[:, sl2 * 512:(sl2 + 1) * 512],
                            in0=pd_sl, scalar=bcol,
                            in1=wshB[:, sl2 * 512:(sl2 + 1) * 512],
                            op0=ALU.mult, op1=ALU.mult),
                            waits=[mm_d[jj][sl2], t_b[j], t_ws, ev_d])
                        ev_d = E("vector", lambda e, xs=xs, sl2=sl2: e.tensor_tensor(
                            out=xs, in0=xs,
                            in1=self.scrD[:, sl2 * 512:(sl2 + 1) * 512], op=ALU.add),
                            waits=[td1])
                    t_group_done = ev_d
        for gg in (gpd, gpu1, gpu0, gpg1, gpg0, gscr, gds1, gds0, gus1,
                   gus0, ggs1, ggs0, gmac2, gmac, gsBp, gsA, gqx, gqt, gtb, gqxT,
                   gwsh, gwsu, gwsg):
            gg.__exit__(None, None, None)
        self.pg.barrier()

    def finale(self):
        E, pg = self.E, self.pg
        if os.environ.get("MK_DEBUG") or os.environ.get("MK_DEBUG2"):
            waits_all = [pg.total("vector"), pg.total("scalar"), pg.total("tensor"),
                         pg.total(self.s_cc), pg.total(self.s_gig), pg.total(self.s_wg)]
            t1 = E("sync", lambda e: e.dma_start(
                out=self.dbg_out, in_=self.dbg.ap()), dma=True, sem=self.s_y,
                waits=waits_all)
            t2 = E("sync", lambda e: e.dma_start(
                out=self.xn_dbg, in_=self.xn_bout.ap()), dma=True, sem=self.s_y,
                waits=waits_all)
            t3 = E("sync", lambda e: e.dma_start(
                out=self.ctx_dbg, in_=self.ctx_bout.ap()), dma=True, sem=self.s_y,
                waits=waits_all)
        t_y = E("sync", lambda e: e.dma_start(
            out=self.y_out.rearrange("(c p) d -> p c d", p=128),
            in_=self.x_res[:, :].rearrange("p (c d) -> p c d", c=TC)),
            dma=True, sem=self.s_y,
            waits=[pg.total("vector"), pg.total("scalar"), pg.total("tensor")])
        pg.wait_only("sync", [t_y])
        pg.wait_only("gpsimd", [t_y])

    def finish(self):
        self.pg.install(self.block)
        self.block_cm.__exit__(None, None, None)
        for g in reversed(self._guards):
            g.__exit__(None, None, None)
        return self.nc


def build_program(n_sub=4):
    mk = MK(n_sub)
    mk.prelude()
    if n_sub >= 2:
        mk.ffn_weights()
    mk.pg.barrier(exclude=(mk.s_cc,))
    if n_sub >= 1:
        mk.attn("a")
    if n_sub >= 2:
        mk.ffn(0)
    if n_sub >= 3:
        mk.attn("b")
    if n_sub >= 4:
        mk.ffn(1)
    mk.finale()
    return mk.finish()



def _shard_inputs(inputs, core):
    b, half = core // 2, core % 2
    heads = [8 * half + j for j in range(HM)]
    W_in = inputs["L0_attn_a_W_in"]
    def qk_cols(base):
        cols = []
        for et in range(2):
            for hh in range(4):
                h = heads[et * 4 + hh]
                cols.extend(base + DH * h + np.arange(0, 64, 2))
        for ot in range(2):
            for hh in range(4):
                h = heads[ot * 4 + hh]
                cols.extend(base + DH * h + np.arange(1, 64, 2))
        for pt in range(4):
            for hh in range(2):
                h = heads[pt * 2 + hh]
                cols.extend(base + DH * h + np.arange(64, 128))
        return cols
    v_cols = []
    for h in heads:
        v_cols.extend(2 * D + DH * h + np.arange(DH))
    win_cols = np.array(qk_cols(0) + qk_cols(D) + v_cols)
    win_s = np.ascontiguousarray(W_in[:, win_cols])

    w_qkv = inputs["L0_attn_b_w_qkv"]
    rows = []
    for base in (0, NH * DH, 2 * NH * DH):
        for h in heads:
            rows.extend(base + DH * h + np.arange(DH))
    wqkvT_s = np.ascontiguousarray(w_qkv[np.array(rows), :].T)

    wg = inputs["L0_ffn_w_gate"]
    wu = inputs["L0_ffn_w_up"]
    wd = inputs["L0_ffn_w_down"]
    down_idx = np.asarray(inputs["L0_ffn_down_reorder_index"])
    inv = np.argsort(down_idx)
    wdp = np.ascontiguousarray(wd[:, inv])

    def pad_shard(w):
        out = np.zeros((ISH, D), dtype=np.float32)
        lo, hi = core * ISH, min((core + 1) * ISH, IF)
        if hi > lo:
            out[0:hi - lo] = w[lo:hi]
        return out

    x = np.ascontiguousarray(inputs["L0_attn_a_X"][b, half * T:(half + 1) * T, :])

    cidx = np.zeros((128, NH), dtype=np.int32)
    p = np.arange(128)
    for gh in range(NH):
        slot, hh = gh // HM, gh % HM
        cidx[:, gh] = ((slot * HM + hh) * 128 + p) * 2 + half

    return {
        "x_in": x,
        "win_s": win_s.astype(np.float32),
        "lnw_in": np.ascontiguousarray(inputs["L0_attn_a_ln_w"]).reshape(1, D).astype(np.float32),
        "lnb_in": np.ascontiguousarray(inputs["L0_attn_a_ln_b"]).reshape(1, D).astype(np.float32),
        "wout_a": np.ascontiguousarray(inputs["L0_attn_a_W_out"]).astype(np.float32),
        "wqkvT_s": wqkvT_s.astype(np.float32),
        "woutT_b": np.ascontiguousarray(inputs["L0_attn_b_w_out"].T).astype(np.float32),
        "wg_s": pad_shard(wg).astype(np.float32),
        "wu_s": pad_shard(wu).astype(np.float32),
        "wd_s": np.ascontiguousarray(
            wdp[core * HSH:(core + 1) * HSH]).astype(np.float32),
        "cidx_in": cidx,
    }


_cache = {}


def kernel(**inputs):
    n_sub = int(os.environ.get("MK_NSUB", "4"))
    nc = build_program(n_sub)
    in_maps = [_shard_inputs(inputs, c) for c in range(NCORE)]
    res = run_bass_kernel_spmd(nc, in_maps, core_ids=list(range(NCORE)))
    out = np.zeros((B, S, D), dtype=np.float32)
    for c in range(NCORE):
        b, half = c // 2, c % 2
        out[b, half * T:(half + 1) * T, :] = res.results[c]["y_out"]
    return out
